# version 34
# baseline (speedup 1.0000x reference)
import dataclasses
from contextlib import ExitStack
import numpy as np
import concourse.bass as bass
import concourse.mybir as mybir
from concourse.bass_utils import run_bass_kernel_spmd

F32 = mybir.dt.float32
BF16 = mybir.dt.bfloat16
AF = mybir.ActivationFunctionType
ALU = mybir.AluOpType
AX = mybir.AxisListType
PI = float(np.pi)

B, S, D = 4, 4096, 1024
NPAIR = 8
NT = 16
NKT = 32
NTOK = NT * 128

PROBE_PERSIST = False
PROBE_ROUTE = False
DEV_PHASES = 4
DEV_ATTN = 4
DEV_SIDES = 3
DEV_CHUNKS = 4
DEV_PROJ = 4
DEV_MERGE = 1
DEV_LN = 1
DEV_SKIP_P2 = 0
DEV_MQ = 1
DEV_NORM = 4
DEV_CWBP = 1
DEV_FP = 5
DEV_HALVES = 2
DEV_NOCWB = 0
DEV_VB = 1
DEV_NGRP = 2
DEV_NPIG = 4


class Sem:
    def __init__(self, nc, name):
        self.h = nc.alloc_semaphore(name)
        self.n = 0


class Prog:
    def __init__(self, nc):
        self.nc = nc
        self.st = {e: [] for e in ("g", "sy", "t", "sc", "ve")}

    def emit(self, eng, fn, inc=None, inc_n=1):
        if inc is not None:
            inc.n += inc_n
        self.st[eng].append(("op", fn, inc, inc_n))
        return inc.n if inc is not None else None

    def wait(self, eng, sem, val):
        if val > 0:
            self.st[eng].append(("wait", sem, val, 0))

    def dma(self, eng, out, in_, sem):
        if sem.n >= 16:
            self.wait(eng, sem, sem.n)
        return self.emit(eng, lambda e, o=out, i=in_: e.dma_start(out=o, in_=i),
                         inc=sem, inc_n=16)

    def mm(self, out, lhsT, rhs, st, sp, inc=None):
        return self.emit("t", lambda e, o=out, l=lhsT, r=rhs, st=st, sp=sp:
                         e.matmul(o, l, r, start=st, stop=sp, skip_group_check=True),
                         inc=inc)

    def run(self):
        nc = self.nc
        with nc.Block() as blk:
            for key, starter in (("g", blk.gpsimd), ("sy", blk.sync), ("t", blk.tensor),
                                 ("sc", blk.scalar), ("ve", blk.vector)):
                def make(k):
                    def f(eng):
                        for item in self.st[k]:
                            if item[0] == "wait":
                                eng.wait_ge(item[1].h, item[2])
                            else:
                                inst = item[1](eng)
                                if item[2] is not None:
                                    inst.then_inc(item[2].h, item[3])
                    return f
                starter(make(key))


def step0(ap_, dims):
    return dataclasses.replace(ap_, ap=list(ap_.ap) + [[0, d] for d in dims])


def bcast(ap_, pre_dims, post):
    return dataclasses.replace(ap_, ap=[ap_.ap[0], [0, pre_dims], [1, post]])


class _Builder:
    _uid = 0

    def _pn(self, base):
        _Builder._uid += 1
        return "%s_u%d" % (base, _Builder._uid)

    def build(self):
        nc = self.nc = bass.Bass("TRN2", target_bir_lowering=False)
        dt_ = nc.dram_tensor
        self.xt = dt_("xt", [D, S], F32, kind="ExternalInput").ap()
        self.xrows = dt_("xrows", [NTOK, D], F32, kind="ExternalInput").ap()
        self.masks_h = dt_("masks", [2, 128, 128], F32, kind="ExternalInput").ap()
        self.wpk = dt_("wpk", [NPAIR, 4, 128, 1024], F32, kind="ExternalInput").ap()
        self.wvt_h = dt_("wvt", [2, 128, 8 * 512], F32, kind="ExternalInput").ap()
        self.wot_h = dt_("wot", [128, 8 * 1024], F32, kind="ExternalInput").ap()
        self.gwt_h = dt_("gwt", [128, 64], F32, kind="ExternalInput").ap()
        self.wg_h = dt_("wg", [9, 128, 8 * 512], F32, kind="ExternalInput").ap()
        self.wu_h = dt_("wu", [9, 128, 8 * 512], F32, kind="ExternalInput").ap()
        self.wd_h = dt_("wd", [9, 128, 4 * 1024], F32, kind="ExternalInput").ap()
        self.slsr_h = dt_("slsr", [1, 16], F32, kind="ExternalInput").ap()
        self.sls2_h = dt_("sls2", [2, 8], F32, kind="ExternalInput").ap()
        self.cbr_h = dt_("cbr", [1, 8], F32, kind="ExternalInput").ap()
        self.ones1_h = dt_("ones1", [1, 128], F32, kind="ExternalInput").ap()
        self.bonesT_h = dt_("bonesT", [2, 128], F32, kind="ExternalInput").ap()
        self.bones_h = dt_("bones", [128, 2], F32, kind="ExternalInput").ap()
        self.basis_h = dt_("basis", [8, 1024], F32, kind="ExternalInput").ap()
        self.onescol_h = dt_("onescol", [128, 1], F32, kind="ExternalInput").ap()
        self.prio4_h = dt_("prio4", [32, 4], F32, kind="ExternalInput").ap()
        self.prio8_h = dt_("prio8", [32, 8], F32, kind="ExternalInput").ap()
        self.y_out = dt_("y", [NTOK, D], F32, kind="ExternalOutput").ap()
        self.rscr = dt_("rscr", [2, 512], BF16, kind="Internal").ap()
        self.wgb = dt_("wgb", [9, 128, 8 * 512], BF16, kind="Internal").ap()
        self.wub = dt_("wub", [9, 128, 8 * 512], BF16, kind="Internal").ap()
        self.wdb = dt_("wdb", [9, 128, 4 * 1024], BF16, kind="Internal").ap()
        self.wob = dt_("wob", [128, 8 * 1024], BF16, kind="Internal").ap()
        if PROBE_PERSIST:
            self.pr_av = dt_("p_avd", [128, 8 * NTOK], F32, kind="ExternalOutput").ap()
            self.pr_cw = dt_("p_cwd", [32, NTOK], F32, kind="ExternalOutput").ap()
            self.pr_xn = dt_("p_xnd", [128, 8 * NTOK], F32, kind="ExternalOutput").ap()
        if PROBE_ROUTE:
            self.pr_sco = dt_("p_sco", [32, NTOK], F32, kind="ExternalOutput").ap()
            self.pr_scc = dt_("p_scc", [32, NTOK], F32, kind="ExternalOutput").ap()
            self.pr_rsm = dt_("p_rsm", [32, 2496], F32, kind="ExternalOutput").ap()
            self.pr_lgt = dt_("p_lgt", [32, NTOK], F32, kind="ExternalOutput").ap()
            self.pr_gw = dt_("p_gw", [128, 64], F32, kind="ExternalOutput").ap()

        al = nc.alloc_sbuf_tensor
        self.ones1 = al("ones1_sb", [1, 128], BF16)
        self.ones1f = al("ones1f_sb", [1, 128], F32)
        self.bonesT = al("bonesT_sb", [2, 128], BF16)
        self.bones = al("bones_sb", [128, 2], BF16)
        self.basis = al("basis_sb", [8, 1024], BF16)
        self.onescol = al("onescol_sb", [128, 1], BF16)
        self.masks = al("masks_sb", [128, 2 * 128], BF16)
        self.prio4 = al("prio4_sb", [32, 4], F32)
        self.prio8 = al("prio8_sb", [32, 8], F32)
        self.slsr = al("slsr_sb", [1, 16], F32)
        self.cbr = al("cbr_sb", [1, 8], F32)
        self.mcols = al("mcols_sb", [128, 16], F32)
        self.lnc2 = al("lnc2_sb", [2, 8], F32)
        self.zero2 = al("zero2_sb", [2, 1], F32)
        self.sls2 = al("sls2_sb", [2, 8], F32)
        self.bias32 = al("bias32_sb", [32, 8], F32)
        self.onecol_f = al("onecol_f", [128, 1], F32)
        self.epscol1 = al("epscol1", [1, 1], F32)
        self.epscol2 = al("epscol2", [2, 1], F32)
        self.pio2col = al("pio2col", [128, 1], F32)
        self.su_er = al("su_er", [1, 16], F32)
        self.su_mn = al("su_mn", [1, 16], F32)
        self.su_cr = al("su_cr", [1, 16], F32)
        self.av_all = al("av_all", [128, 8, NTOK], BF16)
        self.xn_own = al("xn_own", [128, 8, NTOK], BF16)
        self.cw_fm = al("cw_fm", [32, NTOK], BF16)

        P = self.P = Prog(nc)
        self.sG = Sem(nc, "sG")
        self.sSY = Sem(nc, "sSY")
        self.sT = Sem(nc, "sT")
        self.sSC = Sem(nc, "sSC")
        self.sVE = Sem(nc, "sVE")
        self.sOUT = Sem(nc, "sOUT")
        self.sLC = Sem(nc, "sLC")
        self.sMS = Sem(nc, "sMS")

        self.cast_jobs = [(self.wob, self.wot_h)]
        for e in range(9):
            self.cast_jobs += [(self.wgb[e], self.wg_h[e]), (self.wub[e], self.wu_h[e]),
                               (self.wdb[e], self.wd_h[e])]

        self.setup()
        with ExitStack() as es:
            self.xn_oth = es.enter_context(nc.sbuf_tensor("xn_oth", [128, 8, NTOK], BF16))
            self.xn_src = (self.xn_own, self.xn_oth)
            if DEV_PHASES >= 1:
                with ExitStack() as es0:
                    self.p0_ln(es0)
            if DEV_PHASES >= 2 and not DEV_SKIP_P2:
                with ExitStack() as es2:
                    self.p2_routing(es2)
            elif DEV_SKIP_P2:
                self.routing_done = 0
            if DEV_PHASES >= 3:
                with ExitStack() as es1:
                    self.p1_attention(es1)
        if DEV_PHASES >= 4:
            with ExitStack() as es3:
                self.p3_ffn(es3)
        else:
            P.wait("sy", self.sVE, self.sVE.n)
            P.wait("sy", self.sSC, self.sSC.n)
            P.wait("sy", self.sT, self.sT.n)
            P.wait("sy", self.sG, self.sG.n)
            P.wait("sy", self.sSY, self.sSY.n)
            P.dma("sy", self.y_out, self.xrows, self.sOUT)
        if PROBE_PERSIST:
            P.wait("g", self.sOUT, self.sOUT.n)
            P.dma("g", self.pr_av, self.av_all[:].rearrange("p a b -> p (a b)"), self.sOUT)
            P.dma("g", self.pr_cw, self.cw_fm[:], self.sOUT)
            P.dma("g", self.pr_xn, self.xn_own[:].rearrange("p a b -> p (a b)"), self.sOUT)
        P.wait("sy", self.sOUT, self.sOUT.n)
        P.run()
        nc.finalize()
        return nc

    def setup(self):
        nc, P = self.nc, self.P
        sG, sT, sSC, sVE = self.sG, self.sT, self.sSC, self.sVE
        P.dma("g", self.sls2[:], self.sls2_h, sG)
        for dst, src in ((self.ones1, self.ones1_h), (self.bonesT, self.bonesT_h),
                         (self.bones, self.bones_h), (self.basis, self.basis_h),
                         (self.onescol, self.onescol_h), (self.prio4, self.prio4_h),
                         (self.prio8, self.prio8_h), (self.slsr, self.slsr_h),
                         (self.cbr, self.cbr_h)):
            P.dma("g", dst[:], src, sG)
        P.dma("g", self.ones1f[:], self.ones1_h, sG)
        P.dma("g", self.masks[:].rearrange("p (m c) -> p m c", m=2),
              self.masks_h.rearrange("m p c -> p m c"), sG)
        P.emit("g", lambda e: e.memset(self.onecol_f[:], 1.0))
        P.emit("g", lambda e: e.memset(self.epscol1[:], 1e-5))
        P.emit("g", lambda e: e.memset(self.epscol2[:], 1e-6))
        P.emit("g", lambda e: e.memset(self.zero2[:], 0.0))
        P.emit("g", lambda e: e.memset(self.pio2col[:], PI / 2), inc=self.sMS)
        self.setup_ld = sG.n
        with (
            nc.psum_tensor("su_p", [128, 16], F32) as su_p,
            nc.psum_tensor("su_p2", [32, 8], F32) as su_p2,
        ):
            P.wait("sc", sG, self.setup_ld)
            P.wait("sc", self.sMS, 1)
            P.emit("sc", lambda e: e.activation(self.su_er[:], self.slsr[:], AF.Exp), inc=sSC)
            P.wait("ve", sSC, sSC.n)
            P.emit("ve", lambda e: e.tensor_scalar_mul(self.su_mn[:], self.su_er[:], -8.0))
            P.emit("ve", lambda e: e.tensor_scalar_add(self.lnc2[:], self.sls2[:],
                                                       -float(np.log(8.0))), inc=sVE)
            rows_done = sVE.n
            self.c2_done = sVE.n
            P.wait("t", sVE, rows_done)
            P.wait("t", sG, self.setup_ld)
            P.wait("t", self.sMS, 1)
            P.mm(su_p[:], self.ones1f[:], self.su_mn[:], True, True)
            P.mm(su_p2[:], self.ones1f[0:1, 0:32], self.cbr[:], True, True, inc=sT)
            su_mm = sT.n
            P.wait("sc", sT, su_mm)
            P.emit("sc", lambda e: e.activation(self.mcols[:], su_p[:], AF.Copy), inc=sSC)
            P.wait("ve", sT, su_mm)
            P.emit("ve", lambda e: e.tensor_copy(self.bias32[:], su_p2[:]), inc=sVE)
            self.bias32_done = sVE.n

    def p0_ln(self, es):
        nc, P = self.nc, self.P
        sG, sT, sSC, sVE = self.sG, self.sT, self.sSC, self.sVE
        ln_scr = es.enter_context(nc.sbuf_tensor("ln_scr", [128, 2 * 512], BF16))
        ln_mu = es.enter_context(nc.sbuf_tensor("ln_mu", [1, 512], F32))
        ln_qr = es.enter_context(nc.sbuf_tensor("ln_qr", [1, 512], F32))
        ln_var = es.enter_context(nc.sbuf_tensor("ln_var", [1, 512], F32))
        ln_brow = es.enter_context(nc.sbuf_tensor("ln_brow", [1, 512], F32))
        ln_ab = es.enter_context(nc.sbuf_tensor("ln_ab", [1, 2 * 512], BF16))
        p_m = es.enter_context(nc.psum_tensor("p_m", [1, 512], F32))
        p_q = es.enter_context(nc.psum_tensor("p_q", [1, 512], F32))
        p_ab = es.enter_context(nc.psum_tensor("p_ab", [128, 1024], F32))

        xt3 = self.xt.rearrange("(dt p) t -> dt p t", p=128)
        ld_tc = {}
        for tc in range(8):
            xs = self.xn_src[tc // 4]
            c0 = (tc % 4) * 512
            g0 = (tc // 4) * NTOK + c0
            for dt in range(8):
                P.dma("g", xs[:, dt, :][:, c0:c0 + 512], xt3[dt, :, g0:g0 + 512], sG)
            ld_tc[tc] = (sG.n, self.sSY.n)
        P.wait("t", sVE, self.bias32_done)
        if not DEV_LN:
            self.ln_done = 0
            return
        sq_mm_done = [0, 0]
        var_prev = 0
        norm_done = 0
        for tc in range(8):
            src = self.xn_src[tc // 4]
            cols = slice((tc % 4) * 512, (tc % 4) * 512 + 512)
            gld, syld = ld_tc[tc]
            P.wait("sc", sG, gld)
            P.wait("sc", self.sSY, syld)
            P.wait("t", sG, gld)
            P.wait("t", self.sSY, syld)
            P.wait("t", sVE, var_prev)
            for dt in range(8):
                par = dt % 2
                scr = ln_scr[:, par * 512:par * 512 + 512]
                P.wait("sc", sT, sq_mm_done[par])
                sq_done = P.emit("sc", lambda e, s=scr, a=src[:, dt, :][:, cols]:
                                 e.activation(s, a, AF.Square), inc=sSC)
                P.wait("t", sSC, sq_done)
                P.mm(p_m[:], self.onescol[:], src[:, dt, :][:, cols], dt == 0, dt == 7)
                sq_mm_done[par] = P.mm(p_q[:], self.onescol[:], scr, dt == 0, dt == 7, inc=sT)
            stat_mm = sT.n
            mu = ln_mu[:]; qr = ln_qr[:]
            var = ln_var[:]; brow = ln_brow[:]
            a_bf = ln_ab[:, 0:512]; b_bf = ln_ab[:, 512:1024]
            P.wait("ve", sT, stat_mm)
            P.emit("ve", lambda e, mu=mu: e.tensor_scalar_mul(mu, p_m[:], 1.0 / D))
            P.emit("ve", lambda e, qr=qr: e.tensor_scalar_mul(qr, p_q[:], 1.0 / D))
            P.emit("ve", lambda e, mu=mu, var=var: e.tensor_tensor(var, mu, mu, ALU.mult))
            P.emit("ve", lambda e, qr=qr, var=var:
                   e.tensor_tensor(var, qr, var, ALU.subtract), inc=sVE)
            var_prev = sVE.n
            P.wait("sc", sVE, var_prev)
            P.emit("sc", lambda e, var=var: e.activation(var, var, AF.Ln, bias=self.epscol1[:]))
            P.emit("sc", lambda e, var=var, a=a_bf:
                   e.activation(a, var, AF.Exp, scale=-0.5), inc=sSC)
            a_done = sSC.n
            P.wait("ve", sSC, a_done)
            P.emit("ve", lambda e, mu=mu, a=a_bf, brow=brow:
                   e.tensor_tensor(brow, mu, a, ALU.mult))
            P.emit("ve", lambda e, brow=brow, b=b_bf:
                   e.tensor_scalar_mul(b, brow, -1.0), inc=sVE)
            ab_done = sVE.n
            P.wait("t", sVE, ab_done)
            P.wait("t", sVE, norm_done)
            P.mm(p_ab[:, 0:512], self.ones1[:], a_bf, True, True)
            P.mm(p_ab[:, 512:1024], self.ones1[:], b_bf, True, True, inc=sT)
            ab_mm = sT.n
            P.wait("ve", sT, ab_mm)
            for dt in range(8):
                xb = src[:, dt, :][:, cols]
                P.emit("ve", lambda e, xb=xb:
                       e.tensor_tensor(xb, xb, p_ab[:, 0:512], ALU.mult))
                P.emit("ve", lambda e, xb=xb:
                       e.tensor_tensor(xb, xb, p_ab[:, 512:1024], ALU.add), inc=sVE)
            norm_done = sVE.n
        self.ln_done = sVE.n

    def p2_routing(self, es):
        nc, P = self.nc, self.P
        sG, sT, sSC, sVE = self.sG, self.sT, self.sSC, self.sVE
        gw_sb = es.enter_context(nc.sbuf_tensor("gw_sb", [128, 8 * 8], BF16))
        lg = es.enter_context(nc.sbuf_tensor("lg", [32, NTOK], F32))
        lgT = es.enter_context(nc.sbuf_tensor("lgT", [32, NTOK], F32))
        scoT = es.enter_context(nc.sbuf_tensor("scoT", [32, NTOK], F32))
        scc = es.enter_context(nc.sbuf_tensor("scc", [32, NTOK], F32))
        wacc = es.enter_context(nc.sbuf_tensor("wacc", [32, NTOK], F32))
        cwpad = es.enter_context(nc.sbuf_tensor("cwpad", [32, NTOK], F32))
        r_sm = es.enter_context(nc.sbuf_tensor("r_sm", [32, 2496], F32))
        cwp16 = es.enter_context(nc.sbuf_tensor("cwp16", [32, NTOK], F32))
        self._route_dbg = (scoT, scc, r_sm, lgT)
        p_lg = es.enter_context(nc.psum_tensor("p_lg", [8, 512], F32))

        gw_v = gw_sb[:].rearrange("p (kc c) -> p kc c", kc=8)
        P.wait("g", sVE, self.ln_done)
        P.dma("g", gw_sb[:], self.gwt_h, sG)
        gw_ld = sG.n
        P.wait("t", sG, gw_ld)
        P.wait("t", sVE, self.ln_done)
        lgc_done = 0
        for tcc in range(4):
            cols = slice(tcc * 512, tcc * 512 + 512)
            P.wait("t", sSC, lgc_done)
            for kc in range(8):
                P.mm(p_lg[:], gw_v[:, kc, :], self.xn_own[:, kc, :][:, cols],
                     kc == 0, kc == 7, inc=(sT if kc == 7 else None))
            P.wait("sc", sT, sT.n)
            lgc_done = P.emit("sc", lambda e, cols=cols:
                              e.activation(lg[0:8, cols], p_lg[:], AF.Copy), inc=sSC)
        P.wait("ve", sSC, lgc_done)
        P.emit("ve", lambda e: e.transpose(lgT[:], lg[:]), inc=sVE)
        tr1 = sVE.n

        def v3(t):
            return t[:].rearrange("p (j e) -> p j e", e=32)[:, :, 0:8]
        lgv, scov, sccv, waccv, cwv = v3(lgT), v3(scoT), v3(scc), v3(wacc), v3(cwpad)
        P.wait("sc", sVE, tr1)
        sig_done = P.emit("sc", lambda e: e.activation(scov, lgv, AF.Sigmoid), inc=sSC)
        P.wait("ve", sSC, sig_done)
        P.wait("ve", sVE, self.bias32_done)
        bias_b = bcast(self.bias32[:], 64, 8)
        P.emit("ve", lambda e: e.tensor_tensor(sccv, scov, bias_b, ALU.add))
        gs = r_sm[:, 0:256].rearrange("p (j e) -> p j e", e=4)
        m1f = r_sm[:, 256:320]
        m1 = m1f.rearrange("p (j e) -> p j e", e=1)
        eq4 = r_sm[:, 320:576].rearrange("p (j e) -> p j e", e=4)
        sp4 = r_sm[:, 576:832].rearrange("p (j e) -> p j e", e=4)
        selg = [r_sm[:, 832 + i * 256:1088 + i * 256].rearrange("p (j e) -> p j e", e=4)
                for i in range(2)]
        m1b4 = dataclasses.replace(m1f, ap=[m1f.ap[0], [1, 64], [0, 4]])
        m1b8 = dataclasses.replace(m1f, ap=[m1f.ap[0], [1, 64], [0, 8]])
        prio4_b = bcast(self.prio4[:], 64, 4)
        prio8_b = bcast(self.prio8[:], 64, 8)
        P.emit("ve", lambda e: e.tensor_tensor(gs, sccv[:, :, 0:8:2], sccv[:, :, 1:8:2], ALU.add))
        for it in range(2):
            P.emit("ve", lambda e: e.reduce_max(m1, gs, axis=AX.X))
            P.emit("ve", lambda e: e.tensor_tensor(eq4, gs, m1b4, ALU.is_equal))
            P.emit("ve", lambda e: e.tensor_tensor(sp4, eq4, prio4_b, ALU.mult))
            P.emit("ve", lambda e: e.reduce_max(m1, sp4, axis=AX.X))
            P.emit("ve", lambda e, it=it: e.tensor_tensor(selg[it], sp4, m1b4, ALU.is_equal))
            P.emit("ve", lambda e, it=it:
                   e.scalar_tensor_tensor(gs, selg[it], -1e30, gs, ALU.mult, ALU.add))
        gmask = selg[0]
        P.emit("ve", lambda e: e.tensor_tensor(gmask, selg[0], selg[1], ALU.add))
        m8 = r_sm[:, 320:832].rearrange("p (j e) -> p j e", e=8)
        P.emit("ve", lambda e: e.tensor_copy(m8[:, :, 0:8:2], gmask))
        P.emit("ve", lambda e: e.tensor_copy(m8[:, :, 1:8:2], gmask))
        P.emit("ve", lambda e: e.tensor_tensor(waccv, sccv, m8, ALU.mult))
        P.emit("ve", lambda e: e.memset(cwpad[:], 0.0))
        eq8 = r_sm[:, 832:1344].rearrange("p (j e) -> p j e", e=8)
        sp8 = r_sm[:, 1344:1856].rearrange("p (j e) -> p j e", e=8)
        tmp8 = r_sm[:, 1856:2368].rearrange("p (j e) -> p j e", e=8)
        for it in range(4):
            P.emit("ve", lambda e: e.reduce_max(m1, waccv, axis=AX.X))
            P.emit("ve", lambda e: e.tensor_tensor(eq8, waccv, m1b8, ALU.is_equal))
            P.emit("ve", lambda e: e.tensor_tensor(sp8, eq8, prio8_b, ALU.mult))
            P.emit("ve", lambda e: e.reduce_max(m1, sp8, axis=AX.X))
            P.emit("ve", lambda e: e.tensor_tensor(sp8, sp8, m1b8, ALU.is_equal))
            P.emit("ve", lambda e: e.tensor_tensor(tmp8, sp8, scov, ALU.mult))
            P.emit("ve", lambda e: e.tensor_tensor(cwv, cwv, tmp8, ALU.add))
            P.emit("ve", lambda e:
                   e.scalar_tensor_tensor(waccv, sp8, -1e30, waccv, ALU.mult, ALU.add))
        s4 = r_sm[:, 2368:2432]
        s43 = s4.rearrange("p (j e) -> p j e", e=1)
        P.emit("ve", lambda e: e.reduce_sum(s43, cwv, axis=AX.X))
        P.emit("ve", lambda e: e.tensor_scalar_add(s4, s4, 1e-20), inc=sVE)
        s4_done = sVE.n
        inv = r_sm[:, 2432:2496]
        P.wait("sc", sVE, s4_done)
        P.emit("sc", lambda e: e.activation(inv, s4, AF.Ln))
        P.emit("sc", lambda e: e.activation(inv, inv, AF.Exp, scale=-1.0), inc=sSC)
        inv_done = sSC.n
        invb = dataclasses.replace(inv, ap=[inv.ap[0], [1, 64], [0, 8]])
        P.wait("ve", sSC, inv_done)
        P.emit("ve", lambda e: e.scalar_tensor_tensor(cwv, cwv, 2.5, invb, ALU.mult, ALU.mult))
        P.emit("ve", lambda e: e.transpose(cwp16[:], cwpad[:]))
        P.emit("ve", lambda e: e.tensor_copy(self.cw_fm[:], cwp16[:]), inc=sVE)
        self.routing_done = sVE.n
        if PROBE_ROUTE:
            P.wait("g", sVE, self.routing_done)
            P.dma("g", self.pr_sco, scoT[:], self.sOUT)
            P.dma("g", self.pr_scc, scc[:], self.sOUT)
            P.dma("g", self.pr_rsm, r_sm[:], self.sOUT)
            P.dma("g", self.pr_lgt, lgT[:], self.sOUT)
            P.dma("g", self.pr_gw, gw_sb[:], self.sOUT)

    def p1_attention(self, es):
        nc, P = self.nc, self.P
        sG, sT, sVE = self.sG, self.sT, self.sVE
        t = lambda name, shape, dty: es.enter_context(nc.sbuf_tensor(name, shape, dty))
        self.Vb = t("Vb", [128, NKT * 520], BF16)
        self.kcs = (t("kcs0", [128, S], BF16), t("kcs1", [128, S], BF16))
        self.qcs = (t("qcs0", [128, NTOK], BF16), t("qcs1", [128, NTOK], BF16))
        self.ptb = t("ptb", [128, 4 * 1024], BF16)
        self.wpair = t("wpair", [128, 4 * 8 * 128], BF16)
        self.tscr = t("tscr", [128, 9 * 512], BF16)
        self.qcsc = t("qcsc", [128, 2 * 2048], BF16)
        self.msrow = t("msrow", [2, 2 * 512], F32)
        self.msbf = t("msbf", [2, 1024], BF16)
        self.avn = self.tscr[0:64, 2560:3584]
        self.rrow_f = self.msrow[0:1, 0:1024]
        self.rrow = self.msbf[0:1, 512:1024]
        self.avn_dma = 0
        self.wpair_v = self.wpair[:].rearrange("p (m kc c) -> p m kc c", m=4, kc=8)

        P.wait("t", sVE, self.ln_done)
        P.wait("t", self.sSC, self.sSC.n)
        P.wait("ve", sVE, self.c2_done)
        P.wait("g", sVE, self.routing_done)
        P.wait("g", self.sSC, self.sSC.n)
        P.dma("g", self.wpair[:].rearrange("p (m z) -> p m z", m=4),
              self.wpk[0].rearrange("m p z -> p m z"), sG)
        self.wpair_ld = sG.n
        self.merged = 0
        self.merged_g = 0
        self.avn_evac = 0
        self.rrep_dma = 0
        self.rsc_i = 0
        self.avm_prev = 0
        self.pst = dict(amp_evac={}, phi_evac={}, rb_used=0, mrow_done=0, qcs_mult=0,
                        ms_read={})
        self.mcol_sy = {}
        self.mcol_g = {}
        prev_vmm = 0
        for grp in range(DEV_NGRP):
            with nc.sbuf_tensor(self._pn("wv_sb"), [128, 8 * 512], BF16) as wv_sb:
                wv_v = wv_sb[:].rearrange("p (kc c) -> p kc c", kc=8)
                P.wait("g", sT, prev_vmm)
                P.dma("g", wv_sb[:], self.wvt_h[grp], sG)
                wv_ld = sG.n
                if grp == 0:
                    vb_ones = self.Vb[:].rearrange("p (s2 h c) -> p s2 h c",
                                                   s2=NKT, h=8)[:, :, :, 64:65]
                    P.wait("ve", sT, sT.n)
                    P.emit("ve", lambda e, a=vb_ones: e.memset(a, 1.0), inc=sVE)
                    ones_done = sVE.n
                else:
                    P.wait("ve", sT, sT.n)
                pv_evac = {0: 0, 1: 0}
                with (
                    nc.psum_tensor(self._pn("p_v0"), [128, 512], F32) as p_v0,
                    nc.psum_tensor(self._pn("p_v1"), [128, 512], F32) as p_v1,
                ):
                    p_vs = (p_v0, p_v1)
                    if DEV_VB:
                        P.wait("t", sG, wv_ld)
                        P.wait("t", sVE, ones_done)
                        for s_ in range(NKT):
                            src = self.xn_src[s_ // 16]
                            tcol = (s_ % 16) * 128
                            p_v = p_vs[s_ % 2]
                            P.wait("t", sVE, pv_evac[s_ % 2])
                            for kc in range(8):
                                P.mm(p_v[:], src[:, kc, :][:, tcol:tcol + 128],
                                     wv_v[:, kc, :],
                                     kc == 0, kc == 7, inc=(sT if kc == 7 else None))
                            vmm = sT.n
                            dst = self.Vb[:, s_ * 520:s_ * 520 + 520] \
                                .rearrange("p (h c) -> p h c", h=8)[:, :, 0:64]
                            psrc = p_v[:].rearrange("p (h c) -> p h c", h=8)
                            P.wait("ve", sT, vmm)
                            pv_evac[s_ % 2] = P.emit("ve", lambda e, d=dst, s2=psrc:
                                                     e.tensor_copy(d, s2), inc=sVE)
                        prev_vmm = vmm
            self.vb_done = max(pv_evac.values())
            if DEV_ATTN >= 2:
                for pig in range(DEV_NPIG):
                    self.p1_pair(grp, pig)
        self.av_dma_all = self.sSY.n

    def p1_pair(self, grp, pig):
        nc, P = self.nc, self.P
        sG, sT, sSY = self.sG, self.sT, self.sSY
        sSC, sVE = self.sSC, self.sVE
        pair = grp * 4 + pig
        with (
            nc.psum_tensor(self._pn("p_ms"), [2, 512], F32) as p_ms,
            nc.psum_tensor(self._pn("p_rb"), [128, 512], F32) as p_rb,
        ):
            state = self.pst
            pend = None
            with (
                nc.psum_tensor(self._pn("p_a"), [128, 3, 512], F32) as p_a,
                nc.psum_tensor(self._pn("p_p"), [128, 3, 512], F32) as p_p,
            ):
                P.wait("t", sG, self.wpair_ld)
                P.wait("t", sVE, self.vb_done)
                for (isq, si) in ((True, 0), (False, 0), (False, 1))[0:DEV_SIDES]:
                    src = self.xn_own if isq else self.xn_src[si]
                    for cp in range(DEV_CHUNKS // 2):
                        c0 = self.p1_proj_chunk(pair, isq, si, src, 2 * cp,
                                                p_a, p_p, p_ms, p_rb, state)
                        c1 = self.p1_proj_chunk(pair, isq, si, src, 2 * cp + 1,
                                                p_a, p_p, p_ms, p_rb, state)
                        if pend is not None:
                            pend[0][2]()
                            pend[0][3]()
                            pend[1][2]()
                            pend[1][3]()
                        c0[0]()
                        c1[0]()
                        c0[1]()
                        c1[1]()
                        pend = (c0, c1)
            with (
                nc.psum_tensor(self._pn("p_dm"), [128, 512], F32) as _p_dm,
                nc.psum_tensor(self._pn("p_fl"), [128, 512], F32) as p_fl,
            ):
                for _ in range(12):
                    P.mm(p_fl[:], self.masks[:, 0:128], self.xn_own[:, 0, 0:512],
                         True, True)
                pend[0][2]()
                pend[0][3]()
                pend[1][2]()
                pend[1][3]()
                pend = None
        if pair + 1 < NPAIR:
            P.wait("g", sT, sT.n)
            P.dma("g", self.wpair[:].rearrange("p (m z) -> p m z", m=4),
                  self.wpk[pair + 1].rearrange("m p z -> p m z"), sG)
            self.wpair_ld = sG.n
        if DEV_ATTN >= 3:
            self.last_exp = {0: 0, 1: 0}
            for j0 in (0, 4, 8, 12):
                self.p1_strip(pair, pig, j0)

    def p1_proj_chunk(self, pair, isq, si, src, ch, p_a, p_p, p_ms, p_rb, st_):
        P = self.P
        sT, sSC, sVE = self.sT, self.sSC, self.sVE
        tscr = self.tscr
        m_a = 0 if isq else 2
        m_p = 1 if isq else 3
        qc = self.qcsc[:, 0:2048]
        qs = self.qcsc[:, 2048:4096]
        par = ch % 2
        par3 = ch % 3
        cols = slice(ch * 512, ch * 512 + 512)
        escr = tscr[:, 0:512]; csb = escr
        ampb = tscr[:, (1 + par) * 512:(2 + par) * 512]
        asq = tscr[:, (3 + par) * 512:(4 + par) * 512]
        tnh = tscr[:, (5 + par) * 512:(6 + par) * 512]
        snb = tscr[:, (7 + par) * 512:(8 + par) * 512]
        pa = p_a[:, par3, :]
        pp = p_p[:, par3, :]
        P.wait("t", sSC, st_["amp_evac"].get(par3, 0))
        for kc in range(8):
            P.mm(pa, self.wpair_v[:, m_a, kc, :], src[:, kc, :][:, cols],
                 kc == 0, kc == 7, inc=(sT if kc == 7 else None))
        amp_mm = sT.n
        P.wait("t", sSC, st_["phi_evac"].get(par3, 0))
        for kc in range(8):
            P.mm(pp, self.wpair_v[:, m_p, kc, :], src[:, kc, :][:, cols],
                 kc == 0, kc == 7, inc=(sT if kc == 7 else None))
        phi_mm = sT.n
        cell = {}

        def soft():
            P.wait("sc", sT, amp_mm)
            P.wait("sc", sVE, st_["qcs_mult"])
            P.wait("sc", self.sSY, self.avn_dma)
            P.emit("sc", lambda e: e.activation(escr, pa, AF.Exp))
            st_["amp_evac"][par3] = P.emit("sc", lambda e:
                                           e.activation(ampb, escr, AF.Ln,
                                                        bias=self.onecol_f[:]),
                                           inc=sSC)
            P.wait("ve", sSC, st_["amp_evac"][par3])
            P.wait("ve", sT, st_["ms_read"].get(par, 0))
            P.emit("ve", lambda e: e.tensor_tensor(asq, ampb, ampb, ALU.mult), inc=sVE)
            cell["asq_done"] = self.sVE.n

        def trig():
            P.wait("sc", sT, phi_mm)
            tanh_d = P.emit("sc", lambda e: e.activation(tnh, pp, AF.Tanh), inc=sSC)
            st_["phi_evac"][par3] = tanh_d
            P.wait("ve", sSC, tanh_d)
            P.wait("ve", self.sSY, self.mcol_sy.get(ch, 0))
            P.wait("ve", self.sG, self.mcol_g.get(ch, 0))
            abs_d = P.emit("ve", lambda e:
                           e.scalar_tensor_tensor(qc[:, cols], tnh, -1.0, tnh,
                                                  ALU.mult, ALU.max), inc=sVE)
            P.emit("sc", lambda e: e.activation(snb, tnh, AF.Sin, scale=PI))
            P.wait("sc", sVE, abs_d)
            P.emit("sc", lambda e:
                   e.activation(tnh, qc[:, cols], AF.Sin, scale=-PI,
                                bias=self.pio2col[:]),
                   inc=sSC)
            cell["cos_done"] = sSC.n

        def stats():
            P.wait("t", sVE, cell["asq_done"])
            P.wait("t", sSC, st_.get("pms_read", 0))
            st_["ms_read"][par] = P.mm(p_ms[:], self.bones[:], asq, True, True, inc=sT)
            ms_mm = sT.n
            mrow = self.msrow[0:2, 0:512]
            P.wait("sc", sT, ms_mm)
            P.wait("sc", sT, st_["rb_used"])
            P.emit("sc", lambda e: e.activation(mrow, p_ms[:], AF.Ln,
                                                scale=1.0 / 64, bias=self.epscol2[:]))
            bias_c = self.zero2[:] if isq else self.lnc2[:, pair:pair + 1]
            P.emit("sc", lambda e, b=bias_c:
                   e.activation(self.msbf[0:2, 0:512], mrow, AF.Exp,
                                scale=-0.5, bias=b), inc=sSC)
            rstd_done = sSC.n
            st_["pms_read"] = rstd_done
            P.wait("t", sSC, rstd_done)
            P.wait("t", sVE, st_["qcs_mult"])
            st_["rb_used"] = P.mm(p_rb[:], self.bonesT[:], self.msbf[0:2, 0:512],
                                  True, True, inc=sT)

        def qcs_tail():
            P.wait("ve", sSC, cell["cos_done"])
            P.wait("ve", sT, st_["rb_used"])
            P.emit("ve", lambda e: e.tensor_tensor(qc[:, cols], ampb, p_rb[:], ALU.mult))
            P.emit("ve", lambda e: e.tensor_tensor(qs[:, cols], qc[:, cols], snb, ALU.mult))
            P.emit("ve", lambda e: e.tensor_tensor(qc[:, cols], qc[:, cols], tnh, ALU.mult),
                   inc=sVE)
            st_["qcs_mult"] = sVE.n
            if DEV_MERGE:
                dsts, off = (self.qcs, 0) if isq else (self.kcs, si * NTOK)
                P.wait("sy", sVE, st_["qcs_mult"])
                if DEV_MQ:
                    P.wait("g", sVE, st_["qcs_mult"])
                dc = slice(off + ch * 512, off + ch * 512 + 512)
                for h in range(2):
                    P.dma("sy", dsts[h][0:64, dc],
                          self.qcsc[64 * h:64 * h + 64, cols], self.sSY)
                    if DEV_MQ:
                        self.merged_g = P.dma(
                            "g", dsts[h][64:128, dc],
                            self.qcsc[64 * h:64 * h + 64,
                                      2048 + ch * 512:2048 + ch * 512 + 512], self.sG)
                    else:
                        P.dma("sy", dsts[h][64:128, dc],
                              self.qcsc[64 * h:64 * h + 64,
                                        2048 + ch * 512:2048 + ch * 512 + 512], sSY)
                self.merged = self.sSY.n
                self.mcol_sy[ch] = self.merged
                self.mcol_g[ch] = self.merged_g
        return soft, trig, stats, qcs_tail

    def p1_strip(self, pair, pig, j0):
        nc, P = self.nc, self.P
        sT, sSC, sVE, sSY, sG = self.sT, self.sSC, self.sVE, self.sSY, self.sG
        qcols = slice(j0 * 128, j0 * 128 + 512)
        if self.cast_jobs:
            dstc, srcc = self.cast_jobs.pop(0)
            P.dma("g", dstc, srcc, sG)
        with (
            nc.psum_tensor(self._pn("p_av0"), [65, 512], F32) as p_av0,
            nc.psum_tensor(self._pn("p_av1"), [65, 512], F32) as p_av1,
            nc.psum_tensor(self._pn("p_fls"), [128, 512], F32) as p_fls,
        ):
            p_av = (p_av0, p_av1)

            def fill(n):
                for _ in range(n):
                    P.mm(p_fls[:], self.masks[:, 0:128], self.xn_own[:, 0, 0:512],
                         True, True)
            with (
                nc.psum_tensor(self._pn("p_s0"), [128, 2, 512], F32) as p_s0,
                nc.psum_tensor(self._pn("p_s1"), [128, 2, 512], F32) as p_s1,
            ):
                p_s = (p_s0, p_s1)
                P.wait("t", sSY, self.merged)
                P.wait("t", sG, self.merged_g)
                P.wait("t", sVE, self.vb_done)
                P.wait("t", sVE, self.avm_prev)
                P.wait("t", sSC, self.avn_evac)
                n_u = j0 + 4
                exp_done = {}
                mask_done = {}
                av_done = {}
                for u in range(n_u + 1):
                    if u < n_u:
                        w0 = max(0, u - j0) * 128
                        for h in range(2):
                            ps = p_s[h]
                            par = u % 2
                            po = self.ptb[:, (h * 2 + par) * 1024:
                                          (h * 2 + par) * 1024 + 1024] \
                                .rearrange("p (s c) -> p s c", s=2)
                            head = pair * 2 + h
                            P.wait("t", sSC, exp_done.get((u - 1, h), self.last_exp[h]))
                            for src in range(2):
                                slot = src * 16 + u
                                P.mm(ps[:, src, w0:512],
                                     self.kcs[h][:, slot * 128:slot * 128 + 128],
                                     self.qcs[h][:, qcols][:, w0:512],
                                     True, True, inc=(sT if src == 1 else None))
                            s_mm = sT.n
                            P.wait("sc", sT, s_mm)
                            exp_done[(u, h)] = P.emit(
                                "sc", lambda e, ps=ps, po=po, w0=w0, head=head:
                                e.activation(po[:, :, w0:512], ps[:, :, w0:512], AF.Exp,
                                             bias=self.mcols[:, head:head + 1]), inc=sSC)
                            if u >= j0:
                                P.wait("ve", sSC, exp_done[(u, h)])
                                for src in range(2):
                                    P.emit("ve", lambda e, po=po, w0=w0, src=src:
                                           e.tensor_tensor(po[:, src, w0:w0 + 128],
                                                           po[:, src, w0:w0 + 128],
                                                           self.masks[:,
                                                                      src * 128:src * 128 + 128],
                                                           ALU.mult),
                                           inc=(sVE if src == 1 else None))
                                mask_done[(u, h)] = self.sVE.n
                    if u >= 1:
                        ua = u - 1
                        w0 = max(0, ua - j0) * 128
                        for h in range(2):
                            par = ua % 2
                            po = self.ptb[:, (h * 2 + par) * 1024:
                                          (h * 2 + par) * 1024 + 1024] \
                                .rearrange("p (s c) -> p s c", s=2)
                            if ua >= j0:
                                P.wait("t", sVE, mask_done[(ua, h)])
                            for src in range(2):
                                slot = src * 16 + ua
                                vb_l = self.Vb[:, slot * 520 + (pig * 2 + h) * 65:
                                               slot * 520 + (pig * 2 + h) * 65 + 65]
                                st = (ua == 0 and src == 0)
                                sp = (ua == n_u - 1 and src == 1)
                                P.mm(p_av[h][:, w0:512], vb_l, po[:, src, w0:512],
                                     st, sp, inc=(sT if src == 1 else None))
                            if ua == n_u - 1:
                                av_done[h] = sT.n
                self.last_exp = {h: exp_done[(n_u - 1, h)] for h in range(2)}
            for h in range(2):
                with nc.psum_tensor(self._pn("p_bc"), [64, 512], F32) as p_bc:
                    P.wait("sc", sT, av_done[h])
                    P.emit("sc", lambda e, pav=p_av[h]:
                           e.activation(self.rrow_f[:, 0:512], pav[64:65, :], AF.Ln))
                    P.emit("sc", lambda e:
                           e.activation(self.rrow, self.rrow_f[:, 0:512],
                                        AF.Exp, scale=-1.0), inc=sSC)
                    rr_done = sSC.n
                    fill(6)
                    P.wait("t", sSC, rr_done)
                    P.wait("t", sVE, self.avm_prev)
                    P.mm(p_bc[:], self.ones1[:, 0:64], self.rrow,
                         True, True, inc=sT)
                    bc_mm = sT.n
                    P.wait("sc", sT, bc_mm)
                    P.wait("sc", sSY, self.avn_dma)
                    self.avn_evac = P.emit("sc", lambda e, pav=p_av[h]:
                                           e.activation(self.avn[:, 0:512], pav[0:64, :],
                                                        AF.Copy), inc=sSC)
                    P.wait("ve", sSC, self.avn_evac)
                    P.emit("ve", lambda e:
                           e.tensor_tensor(self.avn[:, 0:512], self.avn[:, 0:512],
                                           p_bc[:], ALU.mult), inc=sVE)
                    self.avm_prev = self.sVE.n
                    P.wait("sy", sVE, self.avm_prev)
                    self.avn_dma = P.dma(
                        "sy", self.av_all[:, pair, :][64 * h:64 * h + 64, qcols],
                        self.avn[:, 0:512], sSY)
        self.pst["norm_sc"] = self.sSC.n

    def p3_ffn(self, es):
        nc, P = self.nc, self.P
        sG, sT, sSC, sVE, sSY, sOUT = self.sG, self.sT, self.sSC, self.sVE, self.sSY, self.sOUT
        t = lambda name, shape, dty: es.enter_context(nc.sbuf_tensor(name, shape, dty))
        wo_sb = t("wo_sb", [128, 8 * 1024], BF16)
        wg_sb = t("wg_sb", [128, 2 * 8 * 512], BF16)
        wu_sb = t("wu_sb", [128, 2 * 8 * 512], BF16)
        wd_sb = t("wd_sb", [128, 2 * 4 * 1024], BF16)
        he_e = t("he_e", [128, 2 * 4 * 1024], BF16)
        yacc = t("yacc", [128, 8 * 1024], F32)
        gsil = t("gsil", [128, 2 * 512], BF16)
        hetmp = t("hetmp", [128, 2 * 512], BF16)
        yt = t("yt", [128, 2 * 1024], F32)
        cwb_sb = t("cwb_sb", [128, 2 * 1024], BF16)
        wo_v = wo_sb[:].rearrange("p (ic c) -> p ic c", ic=8)
        wg_v = wg_sb[:].rearrange("p (bb kc c) -> p bb kc c", bb=2, kc=8)
        wu_v = wu_sb[:].rearrange("p (bb kc c) -> p bb kc c", bb=2, kc=8)
        wd_v = wd_sb[:].rearrange("p (bb ic c) -> p bb ic c", bb=2, ic=4)
        he_v = he_e[:].rearrange("p (sb ic c) -> p sb ic c", sb=2, ic=4)
        for q in ("g", "sc", "sy"):
            P.wait(q, sVE, self.sVE.n)
            P.wait(q, self.sSC, self.sSC.n)
            P.wait(q, sSY, self.sSY.n)
            P.wait(q, self.sT, self.sT.n)
        while self.cast_jobs:
            dstc, srcc = self.cast_jobs.pop(0)
            P.dma("g", dstc, srcc, sG)
        cast_all = sG.n
        P.wait("sc", sG, cast_all)
        P.wait("sy", sG, cast_all)
        P.dma("sy", wo_sb[:], self.wob, sSY)
        wo_ld = self.sSY.n

        sched = [(hf, e) for hf in range(2) for e in range(9)]
        wgld, wuld, wdld, gu_last = {}, {}, {}, {}

        def issue_gu(j):
            bb = j % 2
            e = sched[j][1]
            P.wait("g", self.sT, gu_last.get(j - 2, 0))
            wgld[j] = P.dma("g", wg_v[:, bb],
                            self.wgb[e].rearrange("p (kc c) -> p kc c", kc=8), sG)
            P.wait("sc", self.sT, gu_last.get(j - 2, 0))
            wuld[j] = P.dma("sc", wu_v[:, bb],
                            self.wub[e].rearrange("p (kc c) -> p kc c", kc=8), self.sLC)

        def issue_wd(j, guard):
            bb = j % 2
            P.wait("sy", self.sT, guard)
            wdld[j] = P.dma("sy", wd_v[:, bb],
                            self.wdb[sched[j][1]].rearrange("p (ic c) -> p ic c", ic=4),
                            sSY)

        issue_gu(0)
        issue_wd(0, 0)
        issue_gu(1)
        issue_wd(1, 0)
        prev_down = 0
        cwbf_done = self.routing_done
        attn_end_ve = self.sVE.n
        attn_end_sc = self.sSC.n
        with (
            nc.psum_tensor("p_g", [128, 2, 512], F32) as p_g,
            nc.psum_tensor("p_u", [128, 2, 512], F32) as p_u,
            nc.psum_tensor("p_d0", [128, 1024], F32) as p_d0,
            nc.psum_tensor("p_d1", [128, 1024], F32) as p_d1,
        ):
            p_d = (p_d0, p_d1)
            st_ = dict(he_mul={}, down=0, silu_read={}, cwb_evac=0)
            yadd = {0: 0, 1: 0}
            xf_read = {0: 0, 1: 0}
            yt_st = {0: 0, 1: 0}
            eplan = [(0, 1), (2, 3), (4, 5), (6, 7), (8, None)][:DEV_FP]
            for half in range(DEV_HALVES):
                for pi, (ea, eb) in enumerate(eplan):
                    subs = [(0, ea)] + ([(1, eb)] if eb is not None else [])

                    def cwb_pair(pj):
                        for sj, ej in ((0, eplan[pj][0]), (1, eplan[pj][1])):
                            P.wait("t", sVE, max(cwbf_done, attn_end_ve, st_["cwb_evac"],
                                                 *(list(st_["he_mul"].values()) or [0])))
                            P.wait("t", self.sSC, attn_end_sc)
                            for ci in range(2):
                                P.mm(p_g[:, ci, :],
                                     self.basis[:, ej * 128:ej * 128 + 128],
                                     self.cw_fm[0:8, half * 1024 + ci * 512:
                                                half * 1024 + ci * 512 + 512],
                                     True, True, inc=(sT if ci == 1 else None))
                            cwb_mm = sT.n
                            P.wait("ve", sT, cwb_mm)
                            P.emit("ve", lambda e, sj=sj: e.tensor_copy(
                                cwb_sb[:, sj * 1024:sj * 1024 + 1024],
                                p_g[:].rearrange("p a b -> p (a b)")), inc=sVE)
                            st_["cwb_evac"] = self.sVE.n

                    if DEV_NOCWB == 1:
                        pass
                    elif DEV_NOCWB == 2 and pi == 0:
                        cwb_pair(0)
                    elif DEV_CWBP:
                        if pi == 0:
                            cwb_pair(0)
                    elif pi <= 3:
                        cwb_pair(pi)
                    he_ready = 0
                    for sub, e_i in subs:
                        k = half * 9 + e_i
                        bb = k % 2
                        if k + 1 < len(sched) and (k + 1) not in wgld:
                            issue_gu(k + 1)
                        if k + 1 < len(sched) and (k + 1) not in wdld and e_i % 2 == 0:
                            issue_wd(k + 1, prev_down)
                        P.wait("t", sG, wgld[k])
                        P.wait("t", self.sLC, wuld[k])
                        P.wait("ve", sT, st_["down"])
                        for ic in range(4):
                            for tcc in range(2):
                                gpar = (ic * 2 + tcc) % 2
                                ccols = slice(half * 1024 + tcc * 512,
                                              half * 1024 + tcc * 512 + 512)
                                P.wait("t", sVE, st_["he_mul"].get(gpar,
                                                                   st_["cwb_evac"]))
                                P.wait("t", sVE, st_["cwb_evac"])
                                for kc in range(8):
                                    P.mm(p_g[:, gpar, :],
                                         wg_v[:, bb, kc, ic * 128:ic * 128 + 128],
                                         self.xn_own[:, kc, :][:, ccols], kc == 0, kc == 7)
                                    P.mm(p_u[:, gpar, :],
                                         wu_v[:, bb, kc, ic * 128:ic * 128 + 128],
                                         self.xn_own[:, kc, :][:, ccols], kc == 0, kc == 7,
                                         inc=(sT if kc == 7 else None))
                                gu_mm = sT.n
                                gslice = gsil[:, gpar * 512:gpar * 512 + 512]
                                htmp = hetmp[:, gpar * 512:gpar * 512 + 512]
                                P.wait("sc", sT, gu_mm)
                                P.wait("sc", sVE, st_["silu_read"].get(gpar, 0))
                                P.emit("sc", lambda e, g=gslice, pg=p_g[:, gpar, :]:
                                       e.activation(g, pg, AF.Silu), inc=sSC)
                                sil = sSC.n
                                hcols2 = slice(sub * 4096 + ic * 1024 + tcc * 512,
                                               sub * 4096 + ic * 1024 + tcc * 512 + 512)
                                P.wait("ve", sSC, sil)
                                if e_i < 8 and DEV_NOCWB == 0:
                                    P.emit("ve", lambda e, g=gslice, htmp=htmp,
                                           pu=p_u[:, gpar, :]:
                                           e.tensor_tensor(htmp, g, pu, ALU.mult))
                                    st_["he_mul"][gpar] = P.emit(
                                        "ve", lambda e, hc=hcols2, htmp=htmp, tcc=tcc, sub=sub:
                                        e.tensor_tensor(he_e[:, hc], htmp,
                                                        cwb_sb[:, sub * 1024 + tcc * 512:
                                                               sub * 1024 + tcc * 512 + 512],
                                                        ALU.mult), inc=sVE)
                                else:
                                    st_["he_mul"][gpar] = P.emit(
                                        "ve", lambda e, g=gslice, hc=hcols2,
                                        pu=p_u[:, gpar, :]:
                                        e.tensor_tensor(he_e[:, hc], g, pu, ALU.mult),
                                        inc=sVE)
                                st_["silu_read"][gpar] = st_["he_mul"][gpar]
                        gu_last[k] = sT.n
                        he_ready = self.sVE.n
                    if DEV_CWBP and not DEV_NOCWB and pi + 1 <= 3 and pi + 1 < len(eplan):
                        cwb_pair(pi + 1)
                    with_wo = (eb is None)
                    P.wait("t", sSY, max(wdld[half * 9 + ea],
                                         wdld[half * 9 + eb] if eb is not None else 0))
                    for tt in range(8):
                        par = tt % 2
                        pd = p_d[par]
                        P.wait("t", sVE, yadd[par])
                        P.wait("t", sVE, he_ready)
                        for sub, e_i in subs:
                            k = half * 9 + e_i
                            bb = k % 2
                            for ic in range(4):
                                for ci in range(2):
                                    P.mm(pd[:, ci * 512:ci * 512 + 512],
                                         he_v[:, sub, ic, tt * 128:tt * 128 + 128],
                                         wd_v[:, bb, ic, ci * 512:ci * 512 + 512],
                                         (sub == 0 and ic == 0),
                                         (not with_wo) and sub == 1 and ic == 3,
                                         inc=(sT if (sub == len(subs) - 1 and ic == 3
                                                     and ci == 1 and not with_wo)
                                              else None))
                        if with_wo:
                            P.wait("t", sSY, wo_ld)
                            P.wait("t", sSY, self.av_dma_all)
                            for ic in range(8):
                                for ci in range(2):
                                    P.mm(pd[:, ci * 512:ci * 512 + 512],
                                         self.av_all[:, ic, :][:, half * 1024 + tt * 128:
                                                               half * 1024 + tt * 128 + 128],
                                         wo_v[:, ic, ci * 512:ci * 512 + 512],
                                         False, ic == 7,
                                         inc=(sT if (ic == 7 and ci == 1) else None))
                        dmm = sT.n
                        st_["down"] = dmm
                        ya = yacc[:, tt * 1024:tt * 1024 + 1024]
                        P.wait("ve", sT, dmm)
                        if pi == 0:
                            yadd[par] = P.emit("ve", lambda e, ya=ya, pd=pd:
                                               e.tensor_copy(ya, pd[:]), inc=sVE)
                        else:
                            yadd[par] = P.emit("ve", lambda e, ya=ya, pd=pd:
                                               e.tensor_tensor(ya, ya, pd[:], ALU.add),
                                               inc=sVE)
                        if with_wo:
                            yts = yt[:, par * 1024:par * 1024 + 1024]
                            P.wait("sy", sOUT, yt_st[par])
                            P.dma("sy", yts,
                                  self.xrows[(half * 8 + tt) * 128:
                                             (half * 8 + tt) * 128 + 128, :], sSY)
                            xf_ld = sSY.n
                            P.wait("ve", sSY, xf_ld)
                            P.emit("ve", lambda e, ya=ya, yts=yts:
                                   e.tensor_tensor(yts, yts, ya, ALU.add), inc=sVE)
                            P.wait("sy", sVE, self.sVE.n)
                            yt_st[par] = P.dma(
                                "sy", self.y_out[(half * 8 + tt) * 128:
                                                 (half * 8 + tt) * 128 + 128, :],
                                yts, sOUT)
                    prev_down = st_["down"]
                    k_last = half * 9 + (eb if eb is not None else ea)
                    if k_last + 1 < len(sched) and (k_last + 1) not in wdld:
                        issue_wd(k_last + 1, prev_down)


def build_program():
    return _Builder().build()



def kernel(L0_attn_x, L0_attn_Wq_amp, L0_attn_Wk_amp, L0_attn_Wq_phi, L0_attn_Wk_phi,
           L0_attn_Wv, L0_attn_Wo, L0_attn_score_log_scale,
           L0_ffn_gate_w, L0_ffn_correction_bias, L0_ffn_Wg, L0_ffn_Wu, L0_ffn_Wd,
           L0_ffn_Wg_s, L0_ffn_Wu_s, L0_ffn_Wd_s):
    x = np.asarray(L0_attn_x, dtype=np.float32)

    def wt(w):
        return np.ascontiguousarray(np.asarray(w, np.float32).T)

    def sb_tiles(wT, m):
        kc = wT.shape[0] // 128
        return np.ascontiguousarray(wT.reshape(kc, 128, m).transpose(1, 0, 2).reshape(128, kc * m))

    mats = [wt(L0_attn_Wq_amp), wt(L0_attn_Wq_phi), wt(L0_attn_Wk_amp), wt(L0_attn_Wk_phi)]
    wpk = np.empty((NPAIR, 4, 128, 1024), np.float32)
    for pr in range(NPAIR):
        for mi, wT in enumerate(mats):
            wpk[pr, mi] = sb_tiles(np.ascontiguousarray(wT[:, pr * 128:(pr + 1) * 128]), 128)
    wvt_f = wt(L0_attn_Wv)
    wvt = np.stack([sb_tiles(np.ascontiguousarray(wvt_f[:, g * 512:(g + 1) * 512]), 512)
                    for g in range(2)])
    wot = sb_tiles(wt(L0_attn_Wo), 1024)
    gwt = sb_tiles(wt(L0_ffn_gate_w), 8)
    wg = np.stack([sb_tiles(w, 512) for w in
                   list(np.asarray(L0_ffn_Wg, np.float32)) + [np.asarray(L0_ffn_Wg_s, np.float32)]])
    wu = np.stack([sb_tiles(w, 512) for w in
                   list(np.asarray(L0_ffn_Wu, np.float32)) + [np.asarray(L0_ffn_Wu_s, np.float32)]])
    wd = np.stack([sb_tiles(w, 1024) for w in
                   list(np.asarray(L0_ffn_Wd, np.float32)) + [np.asarray(L0_ffn_Wd_s, np.float32)]])
    slsr = np.ascontiguousarray(np.asarray(L0_attn_score_log_scale, np.float32).reshape(1, 16))
    sls2 = np.ascontiguousarray(np.asarray(L0_attn_score_log_scale, np.float32)
                                .reshape(8, 2).T)
    cbr = np.ascontiguousarray(np.asarray(L0_ffn_correction_bias, np.float32).reshape(1, 8))

    tri = (np.arange(128)[None, :] >= np.arange(128)[:, None]).astype(np.float32)
    bones = np.zeros((128, 2), np.float32)
    bones[0:64, 0] = 1.0
    bones[64:128, 1] = 1.0
    bonesT = np.ascontiguousarray(bones.T)
    basis = np.zeros((8, 1024), np.float32)
    for e in range(8):
        basis[e, e * 128:(e + 1) * 128] = 1.0

    nc = build_program()

    in_maps = []
    for core in range(8):
        b, r = core // 2, core % 2
        tiles = x[b].reshape(32, 128, D)
        order = [2 * j + r for j in range(16)] + [2 * j + (1 - r) for j in range(16)]
        xsplit = tiles[order].reshape(S, D)
        im = dict(
            xt=np.ascontiguousarray(xsplit.T),
            xrows=np.ascontiguousarray(xsplit[0:NTOK]),
            masks=np.ascontiguousarray(np.stack(
                [tri, np.zeros((128, 128), np.float32) if r == 0
                 else np.ones((128, 128), np.float32)])),
            wpk=wpk, wvt=wvt, wot=wot, gwt=gwt, wg=wg, wu=wu, wd=wd,
            slsr=slsr, sls2=sls2, cbr=cbr,
            ones1=np.ones((1, 128), np.float32),
            bonesT=bonesT, bones=bones, basis=basis,
            onescol=np.ones((128, 1), np.float32),
            prio4=np.ascontiguousarray(np.broadcast_to(
                np.arange(4, 0, -1, dtype=np.float32)[None, :], (32, 4))),
            prio8=np.ascontiguousarray(np.broadcast_to(
                np.arange(8, 0, -1, dtype=np.float32)[None, :], (32, 8))),
        )
        in_maps.append(im)

    res = run_bass_kernel_spmd(nc, in_maps, core_ids=list(range(8)))
    globals()['LAST_RESULTS'] = res.results

    out = np.empty((B, S, D), np.float32)
    for core in range(8):
        b, r = core // 2, core % 2
        y = res.results[core]["y"].reshape(16, 128, D)
        for j in range(16):
            out[b, (2 * j + r) * 128:(2 * j + r) * 128 + 128, :] = y[j]
    return out


# version 35
# speedup vs baseline: 1.0786x; 1.0786x over previous
import dataclasses
from contextlib import ExitStack
import numpy as np
import concourse.bass as bass
import concourse.mybir as mybir
from concourse.bass_utils import run_bass_kernel_spmd

F32 = mybir.dt.float32
BF16 = mybir.dt.bfloat16
AF = mybir.ActivationFunctionType
ALU = mybir.AluOpType
AX = mybir.AxisListType
PI = float(np.pi)

B, S, D = 4, 4096, 1024
NPAIR = 8
NT = 16
NKT = 32
NTOK = NT * 128

PROBE_PERSIST = False
PROBE_ROUTE = False
DEV_PHASES = 4
DEV_ATTN = 4
DEV_SIDES = 3
DEV_CHUNKS = 4
DEV_PROJ = 4
DEV_MERGE = 1
DEV_LN = 1
DEV_SKIP_P2 = 0
DEV_MQ = 1
DEV_NORM = 4
DEV_CWBP = 1
DEV_FP = 5
DEV_HALVES = 2
DEV_NOCWB = 0
DEV_VB = 1
DEV_NGRP = 2
DEV_NPIG = 4


class Sem:
    def __init__(self, nc, name):
        self.h = nc.alloc_semaphore(name)
        self.n = 0


class Prog:
    def __init__(self, nc):
        self.nc = nc
        self.st = {e: [] for e in ("g", "sy", "t", "sc", "ve")}

    def emit(self, eng, fn, inc=None, inc_n=1):
        if inc is not None:
            inc.n += inc_n
        self.st[eng].append(("op", fn, inc, inc_n))
        return inc.n if inc is not None else None

    def wait(self, eng, sem, val):
        if val > 0:
            self.st[eng].append(("wait", sem, val, 0))

    def dma(self, eng, out, in_, sem):
        return self.emit(eng, lambda e, o=out, i=in_: e.dma_start(out=o, in_=i),
                         inc=sem, inc_n=16)

    def mm(self, out, lhsT, rhs, st, sp, inc=None):
        return self.emit("t", lambda e, o=out, l=lhsT, r=rhs, st=st, sp=sp:
                         e.matmul(o, l, r, start=st, stop=sp, skip_group_check=True),
                         inc=inc)

    def run(self):
        nc = self.nc
        with nc.Block() as blk:
            for key, starter in (("g", blk.gpsimd), ("sy", blk.sync), ("t", blk.tensor),
                                 ("sc", blk.scalar), ("ve", blk.vector)):
                def make(k):
                    def f(eng):
                        for item in self.st[k]:
                            if item[0] == "wait":
                                eng.wait_ge(item[1].h, item[2])
                            else:
                                inst = item[1](eng)
                                if item[2] is not None:
                                    inst.then_inc(item[2].h, item[3])
                    return f
                starter(make(key))


def step0(ap_, dims):
    return dataclasses.replace(ap_, ap=list(ap_.ap) + [[0, d] for d in dims])


def bcast(ap_, pre_dims, post):
    return dataclasses.replace(ap_, ap=[ap_.ap[0], [0, pre_dims], [1, post]])


class _Builder:
    _uid = 0

    def _pn(self, base):
        _Builder._uid += 1
        return "%s_u%d" % (base, _Builder._uid)

    def build(self):
        nc = self.nc = bass.Bass("TRN2", target_bir_lowering=False)
        dt_ = nc.dram_tensor
        self.xt = dt_("xt", [D, S], F32, kind="ExternalInput").ap()
        self.xrows = dt_("xrows", [NTOK, D], F32, kind="ExternalInput").ap()
        self.masks_h = dt_("masks", [2, 128, 128], F32, kind="ExternalInput").ap()
        self.wpk = dt_("wpk", [NPAIR, 4, 128, 1024], F32, kind="ExternalInput").ap()
        self.wvt_h = dt_("wvt", [2, 128, 8 * 512], F32, kind="ExternalInput").ap()
        self.wot_h = dt_("wot", [128, 8 * 1024], F32, kind="ExternalInput").ap()
        self.gwt_h = dt_("gwt", [128, 64], F32, kind="ExternalInput").ap()
        self.wg_h = dt_("wg", [9, 128, 8 * 512], F32, kind="ExternalInput").ap()
        self.wu_h = dt_("wu", [9, 128, 8 * 512], F32, kind="ExternalInput").ap()
        self.wd_h = dt_("wd", [9, 128, 4 * 1024], F32, kind="ExternalInput").ap()
        self.slsr_h = dt_("slsr", [1, 16], F32, kind="ExternalInput").ap()
        self.sls2_h = dt_("sls2", [2, 8], F32, kind="ExternalInput").ap()
        self.cbr_h = dt_("cbr", [1, 8], F32, kind="ExternalInput").ap()
        self.ones1_h = dt_("ones1", [1, 128], F32, kind="ExternalInput").ap()
        self.bonesT_h = dt_("bonesT", [2, 128], F32, kind="ExternalInput").ap()
        self.bones_h = dt_("bones", [128, 2], F32, kind="ExternalInput").ap()
        self.basis_h = dt_("basis", [8, 1024], F32, kind="ExternalInput").ap()
        self.onescol_h = dt_("onescol", [128, 1], F32, kind="ExternalInput").ap()
        self.prio4_h = dt_("prio4", [32, 4], F32, kind="ExternalInput").ap()
        self.prio8_h = dt_("prio8", [32, 8], F32, kind="ExternalInput").ap()
        self.y_out = dt_("y", [NTOK, D], F32, kind="ExternalOutput").ap()
        self.rscr = dt_("rscr", [2, 512], BF16, kind="Internal").ap()
        self.wgb = dt_("wgb", [9, 128, 8 * 512], BF16, kind="Internal").ap()
        self.wub = dt_("wub", [9, 128, 8 * 512], BF16, kind="Internal").ap()
        self.wdb = dt_("wdb", [9, 128, 4 * 1024], BF16, kind="Internal").ap()
        self.wob = dt_("wob", [128, 8 * 1024], BF16, kind="Internal").ap()
        if PROBE_PERSIST:
            self.pr_av = dt_("p_avd", [128, 8 * NTOK], F32, kind="ExternalOutput").ap()
            self.pr_cw = dt_("p_cwd", [32, NTOK], F32, kind="ExternalOutput").ap()
            self.pr_xn = dt_("p_xnd", [128, 8 * NTOK], F32, kind="ExternalOutput").ap()
        if PROBE_ROUTE:
            self.pr_sco = dt_("p_sco", [32, NTOK], F32, kind="ExternalOutput").ap()
            self.pr_scc = dt_("p_scc", [32, NTOK], F32, kind="ExternalOutput").ap()
            self.pr_rsm = dt_("p_rsm", [32, 2496], F32, kind="ExternalOutput").ap()
            self.pr_lgt = dt_("p_lgt", [32, NTOK], F32, kind="ExternalOutput").ap()
            self.pr_gw = dt_("p_gw", [128, 64], F32, kind="ExternalOutput").ap()

        al = nc.alloc_sbuf_tensor
        self.ones1 = al("ones1_sb", [1, 128], BF16)
        self.ones1f = al("ones1f_sb", [1, 128], F32)
        self.bonesT = al("bonesT_sb", [2, 128], BF16)
        self.bones = al("bones_sb", [128, 2], BF16)
        self.basis = al("basis_sb", [8, 1024], BF16)
        self.onescol = al("onescol_sb", [128, 1], BF16)
        self.masks = al("masks_sb", [128, 2 * 128], BF16)
        self.prio4 = al("prio4_sb", [32, 4], F32)
        self.prio8 = al("prio8_sb", [32, 8], F32)
        self.slsr = al("slsr_sb", [1, 16], F32)
        self.cbr = al("cbr_sb", [1, 8], F32)
        self.mcols = al("mcols_sb", [128, 16], F32)
        self.lnc2 = al("lnc2_sb", [2, 8], F32)
        self.zero2 = al("zero2_sb", [2, 1], F32)
        self.sls2 = al("sls2_sb", [2, 8], F32)
        self.bias32 = al("bias32_sb", [32, 8], F32)
        self.onecol_f = al("onecol_f", [128, 1], F32)
        self.epscol1 = al("epscol1", [1, 1], F32)
        self.epscol2 = al("epscol2", [2, 1], F32)
        self.pio2col = al("pio2col", [128, 1], F32)
        self.su_er = al("su_er", [1, 16], F32)
        self.su_mn = al("su_mn", [1, 16], F32)
        self.su_cr = al("su_cr", [1, 16], F32)
        self.av_all = al("av_all", [128, 8, NTOK], BF16)
        self.xn_own = al("xn_own", [128, 8, NTOK], BF16)
        self.cw_fm = al("cw_fm", [32, NTOK], BF16)

        P = self.P = Prog(nc)
        self.sG = Sem(nc, "sG")
        self.sSY = Sem(nc, "sSY")
        self.sT = Sem(nc, "sT")
        self.sSC = Sem(nc, "sSC")
        self.sVE = Sem(nc, "sVE")
        self.sOUT = Sem(nc, "sOUT")
        self.sLC = Sem(nc, "sLC")

        self.cast_jobs = [(self.wob, self.wot_h)]
        for e in range(9):
            self.cast_jobs += [(self.wgb[e], self.wg_h[e]), (self.wub[e], self.wu_h[e]),
                               (self.wdb[e], self.wd_h[e])]

        self.setup()
        with ExitStack() as es:
            self.xn_oth = es.enter_context(nc.sbuf_tensor("xn_oth", [128, 8, NTOK], BF16))
            self.xn_src = (self.xn_own, self.xn_oth)
            if DEV_PHASES >= 1:
                with ExitStack() as es0:
                    self.p0_ln(es0)
            if DEV_PHASES >= 2 and not DEV_SKIP_P2:
                with ExitStack() as es2:
                    self.p2_routing(es2)
            elif DEV_SKIP_P2:
                self.routing_done = 0
            if DEV_PHASES >= 3:
                with ExitStack() as es1:
                    self.p1_attention(es1)
        if DEV_PHASES >= 4:
            with ExitStack() as es3:
                self.p3_ffn(es3)
        else:
            P.wait("sy", self.sVE, self.sVE.n)
            P.wait("sy", self.sSC, self.sSC.n)
            P.wait("sy", self.sT, self.sT.n)
            P.wait("sy", self.sG, self.sG.n)
            P.wait("sy", self.sSY, self.sSY.n)
            P.dma("sy", self.y_out, self.xrows, self.sOUT)
        if PROBE_PERSIST:
            P.wait("g", self.sOUT, self.sOUT.n)
            P.dma("g", self.pr_av, self.av_all[:].rearrange("p a b -> p (a b)"), self.sOUT)
            P.dma("g", self.pr_cw, self.cw_fm[:], self.sOUT)
            P.dma("g", self.pr_xn, self.xn_own[:].rearrange("p a b -> p (a b)"), self.sOUT)
        P.wait("sy", self.sOUT, self.sOUT.n)
        P.run()
        nc.finalize()
        return nc

    def setup(self):
        nc, P = self.nc, self.P
        sG, sT, sSC, sVE = self.sG, self.sT, self.sSC, self.sVE
        P.dma("g", self.sls2[:], self.sls2_h, sG)
        for dst, src in ((self.ones1, self.ones1_h), (self.bonesT, self.bonesT_h),
                         (self.bones, self.bones_h), (self.basis, self.basis_h),
                         (self.onescol, self.onescol_h), (self.prio4, self.prio4_h),
                         (self.prio8, self.prio8_h), (self.slsr, self.slsr_h),
                         (self.cbr, self.cbr_h)):
            P.dma("g", dst[:], src, sG)
        P.dma("g", self.ones1f[:], self.ones1_h, sG)
        P.dma("g", self.masks[:].rearrange("p (m c) -> p m c", m=2),
              self.masks_h.rearrange("m p c -> p m c"), sG)
        P.emit("g", lambda e: e.memset(self.onecol_f[:], 1.0))
        P.emit("g", lambda e: e.memset(self.epscol1[:], 1e-5))
        P.emit("g", lambda e: e.memset(self.epscol2[:], 1e-6))
        P.emit("g", lambda e: e.memset(self.zero2[:], 0.0))
        P.emit("g", lambda e: e.memset(self.pio2col[:], PI / 2), inc=sG)
        self.setup_ld = sG.n
        with (
            nc.psum_tensor("su_p", [128, 16], F32) as su_p,
            nc.psum_tensor("su_p2", [32, 8], F32) as su_p2,
        ):
            P.wait("sc", sG, self.setup_ld)
            P.emit("sc", lambda e: e.activation(self.su_er[:], self.slsr[:], AF.Exp), inc=sSC)
            P.wait("ve", sSC, sSC.n)
            P.emit("ve", lambda e: e.tensor_scalar_mul(self.su_mn[:], self.su_er[:], -8.0))
            P.emit("ve", lambda e: e.tensor_scalar_add(self.lnc2[:], self.sls2[:],
                                                       -float(np.log(8.0))), inc=sVE)
            rows_done = sVE.n
            self.c2_done = sVE.n
            P.wait("t", sVE, rows_done)
            P.wait("t", sG, self.setup_ld)
            P.mm(su_p[:], self.ones1f[:], self.su_mn[:], True, True)
            P.mm(su_p2[:], self.ones1f[0:1, 0:32], self.cbr[:], True, True, inc=sT)
            su_mm = sT.n
            P.wait("sc", sT, su_mm)
            P.emit("sc", lambda e: e.activation(self.mcols[:], su_p[:], AF.Copy), inc=sSC)
            P.wait("ve", sT, su_mm)
            P.emit("ve", lambda e: e.tensor_copy(self.bias32[:], su_p2[:]), inc=sVE)
            self.bias32_done = sVE.n

    def p0_ln(self, es):
        nc, P = self.nc, self.P
        sG, sT, sSC, sVE = self.sG, self.sT, self.sSC, self.sVE
        ln_scr = es.enter_context(nc.sbuf_tensor("ln_scr", [128, 2 * 512], BF16))
        ln_rows = es.enter_context(nc.sbuf_tensor("ln_rows", [1, 4 * 512], F32))
        ln_ab = es.enter_context(nc.sbuf_tensor("ln_ab", [1, 2 * 512], BF16))
        p_m = es.enter_context(nc.psum_tensor("p_m", [1, 512], F32))
        p_q = es.enter_context(nc.psum_tensor("p_q", [1, 512], F32))
        p_ab = es.enter_context(nc.psum_tensor("p_ab", [128, 1024], F32))

        xt3 = self.xt.rearrange("(dt p) t -> dt p t", p=128)
        ld_tc = {}
        for tc in range(8):
            xs = self.xn_src[tc // 4]
            c0 = (tc % 4) * 512
            g0 = (tc // 4) * NTOK + c0
            for dt in range(8):
                P.dma("g", xs[:, dt, :][:, c0:c0 + 512], xt3[dt, :, g0:g0 + 512], sG)
            ld_tc[tc] = (sG.n, self.sSY.n)
        P.wait("t", sVE, self.bias32_done)
        if not DEV_LN:
            self.ln_done = 0
            return
        sq_mm_done = [0, 0]
        var_prev = 0
        norm_done = 0
        for tc in range(8):
            src = self.xn_src[tc // 4]
            cols = slice((tc % 4) * 512, (tc % 4) * 512 + 512)
            gld, syld = ld_tc[tc]
            P.wait("sc", sG, gld)
            P.wait("sc", self.sSY, syld)
            P.wait("t", sG, gld)
            P.wait("t", self.sSY, syld)
            P.wait("t", sVE, var_prev)
            for dt in range(8):
                par = dt % 2
                scr = ln_scr[:, par * 512:par * 512 + 512]
                P.wait("sc", sT, sq_mm_done[par])
                sq_done = P.emit("sc", lambda e, s=scr, a=src[:, dt, :][:, cols]:
                                 e.activation(s, a, AF.Square), inc=sSC)
                P.wait("t", sSC, sq_done)
                P.mm(p_m[:], self.onescol[:], src[:, dt, :][:, cols], dt == 0, dt == 7)
                sq_mm_done[par] = P.mm(p_q[:], self.onescol[:], scr, dt == 0, dt == 7, inc=sT)
            stat_mm = sT.n
            mu = ln_rows[:, 0:512]; qr = ln_rows[:, 512:1024]
            var = ln_rows[:, 1024:1536]; brow = ln_rows[:, 1536:2048]
            a_bf = ln_ab[:, 0:512]; b_bf = ln_ab[:, 512:1024]
            P.wait("ve", sT, stat_mm)
            P.emit("ve", lambda e, mu=mu: e.tensor_scalar_mul(mu, p_m[:], 1.0 / D))
            P.emit("ve", lambda e, qr=qr: e.tensor_scalar_mul(qr, p_q[:], 1.0 / D))
            P.emit("ve", lambda e, mu=mu, var=var: e.tensor_tensor(var, mu, mu, ALU.mult))
            P.emit("ve", lambda e, qr=qr, var=var:
                   e.tensor_tensor(var, qr, var, ALU.subtract), inc=sVE)
            var_prev = sVE.n
            P.wait("sc", sVE, var_prev)
            P.emit("sc", lambda e, var=var: e.activation(var, var, AF.Ln, bias=self.epscol1[:]))
            P.emit("sc", lambda e, var=var, a=a_bf:
                   e.activation(a, var, AF.Exp, scale=-0.5), inc=sSC)
            a_done = sSC.n
            P.wait("ve", sSC, a_done)
            P.emit("ve", lambda e, mu=mu, a=a_bf, brow=brow:
                   e.tensor_tensor(brow, mu, a, ALU.mult))
            P.emit("ve", lambda e, brow=brow, b=b_bf:
                   e.tensor_scalar_mul(b, brow, -1.0), inc=sVE)
            ab_done = sVE.n
            P.wait("t", sVE, ab_done)
            P.wait("t", sVE, norm_done)
            P.mm(p_ab[:, 0:512], self.ones1[:], a_bf, True, True)
            P.mm(p_ab[:, 512:1024], self.ones1[:], b_bf, True, True, inc=sT)
            ab_mm = sT.n
            P.wait("ve", sT, ab_mm)
            for dt in range(8):
                xb = src[:, dt, :][:, cols]
                P.emit("ve", lambda e, xb=xb:
                       e.tensor_tensor(xb, xb, p_ab[:, 0:512], ALU.mult))
                P.emit("ve", lambda e, xb=xb:
                       e.tensor_tensor(xb, xb, p_ab[:, 512:1024], ALU.add), inc=sVE)
            norm_done = sVE.n
        self.ln_done = sVE.n

    def p2_routing(self, es):
        nc, P = self.nc, self.P
        sG, sT, sSC, sVE = self.sG, self.sT, self.sSC, self.sVE
        gw_sb = es.enter_context(nc.sbuf_tensor("gw_sb", [128, 8 * 8], BF16))
        lg = es.enter_context(nc.sbuf_tensor("lg", [32, NTOK], F32))
        lgT = es.enter_context(nc.sbuf_tensor("lgT", [32, NTOK], F32))
        scoT = es.enter_context(nc.sbuf_tensor("scoT", [32, NTOK], F32))
        scc = es.enter_context(nc.sbuf_tensor("scc", [32, NTOK], F32))
        wacc = es.enter_context(nc.sbuf_tensor("wacc", [32, NTOK], F32))
        cwpad = es.enter_context(nc.sbuf_tensor("cwpad", [32, NTOK], F32))
        r_sm = es.enter_context(nc.sbuf_tensor("r_sm", [32, 2496], F32))
        cwp16 = es.enter_context(nc.sbuf_tensor("cwp16", [32, NTOK], F32))
        self._route_dbg = (scoT, scc, r_sm, lgT)
        p_lg = es.enter_context(nc.psum_tensor("p_lg", [8, 512], F32))

        gw_v = gw_sb[:].rearrange("p (kc c) -> p kc c", kc=8)
        P.wait("g", sVE, self.ln_done)
        P.dma("g", gw_sb[:], self.gwt_h, sG)
        gw_ld = sG.n
        P.wait("t", sG, gw_ld)
        P.wait("t", sVE, self.ln_done)
        lgc_done = 0
        for tcc in range(4):
            cols = slice(tcc * 512, tcc * 512 + 512)
            P.wait("t", sSC, lgc_done)
            for kc in range(8):
                P.mm(p_lg[:], gw_v[:, kc, :], self.xn_own[:, kc, :][:, cols],
                     kc == 0, kc == 7, inc=(sT if kc == 7 else None))
            P.wait("sc", sT, sT.n)
            lgc_done = P.emit("sc", lambda e, cols=cols:
                              e.activation(lg[0:8, cols], p_lg[:], AF.Copy), inc=sSC)
        P.wait("ve", sSC, lgc_done)
        P.emit("ve", lambda e: e.transpose(lgT[:], lg[:]), inc=sVE)
        tr1 = sVE.n

        def v3(t):
            return t[:].rearrange("p (j e) -> p j e", e=32)[:, :, 0:8]
        lgv, scov, sccv, waccv, cwv = v3(lgT), v3(scoT), v3(scc), v3(wacc), v3(cwpad)
        P.wait("sc", sVE, tr1)
        sig_done = P.emit("sc", lambda e: e.activation(scov, lgv, AF.Sigmoid), inc=sSC)
        P.wait("ve", sSC, sig_done)
        P.wait("ve", sVE, self.bias32_done)
        bias_b = bcast(self.bias32[:], 64, 8)
        P.emit("ve", lambda e: e.tensor_tensor(sccv, scov, bias_b, ALU.add))
        gs = r_sm[:, 0:256].rearrange("p (j e) -> p j e", e=4)
        m1f = r_sm[:, 256:320]
        m1 = m1f.rearrange("p (j e) -> p j e", e=1)
        eq4 = r_sm[:, 320:576].rearrange("p (j e) -> p j e", e=4)
        sp4 = r_sm[:, 576:832].rearrange("p (j e) -> p j e", e=4)
        selg = [r_sm[:, 832 + i * 256:1088 + i * 256].rearrange("p (j e) -> p j e", e=4)
                for i in range(2)]
        m1b4 = dataclasses.replace(m1f, ap=[m1f.ap[0], [1, 64], [0, 4]])
        m1b8 = dataclasses.replace(m1f, ap=[m1f.ap[0], [1, 64], [0, 8]])
        prio4_b = bcast(self.prio4[:], 64, 4)
        prio8_b = bcast(self.prio8[:], 64, 8)
        P.emit("ve", lambda e: e.tensor_tensor(gs, sccv[:, :, 0:8:2], sccv[:, :, 1:8:2], ALU.add))
        for it in range(2):
            P.emit("ve", lambda e: e.reduce_max(m1, gs, axis=AX.X))
            P.emit("ve", lambda e: e.tensor_tensor(eq4, gs, m1b4, ALU.is_equal))
            P.emit("ve", lambda e: e.tensor_tensor(sp4, eq4, prio4_b, ALU.mult))
            P.emit("ve", lambda e: e.reduce_max(m1, sp4, axis=AX.X))
            P.emit("ve", lambda e, it=it: e.tensor_tensor(selg[it], sp4, m1b4, ALU.is_equal))
            P.emit("ve", lambda e, it=it:
                   e.scalar_tensor_tensor(gs, selg[it], -1e30, gs, ALU.mult, ALU.add))
        gmask = selg[0]
        P.emit("ve", lambda e: e.tensor_tensor(gmask, selg[0], selg[1], ALU.add))
        m8 = r_sm[:, 320:832].rearrange("p (j e) -> p j e", e=8)
        P.emit("ve", lambda e: e.tensor_copy(m8[:, :, 0:8:2], gmask))
        P.emit("ve", lambda e: e.tensor_copy(m8[:, :, 1:8:2], gmask))
        P.emit("ve", lambda e: e.tensor_tensor(waccv, sccv, m8, ALU.mult))
        P.emit("ve", lambda e: e.memset(cwpad[:], 0.0))
        eq8 = r_sm[:, 832:1344].rearrange("p (j e) -> p j e", e=8)
        sp8 = r_sm[:, 1344:1856].rearrange("p (j e) -> p j e", e=8)
        tmp8 = r_sm[:, 1856:2368].rearrange("p (j e) -> p j e", e=8)
        for it in range(4):
            P.emit("ve", lambda e: e.reduce_max(m1, waccv, axis=AX.X))
            P.emit("ve", lambda e: e.tensor_tensor(eq8, waccv, m1b8, ALU.is_equal))
            P.emit("ve", lambda e: e.tensor_tensor(sp8, eq8, prio8_b, ALU.mult))
            P.emit("ve", lambda e: e.reduce_max(m1, sp8, axis=AX.X))
            P.emit("ve", lambda e: e.tensor_tensor(sp8, sp8, m1b8, ALU.is_equal))
            P.emit("ve", lambda e: e.tensor_tensor(tmp8, sp8, scov, ALU.mult))
            P.emit("ve", lambda e: e.tensor_tensor(cwv, cwv, tmp8, ALU.add))
            P.emit("ve", lambda e:
                   e.scalar_tensor_tensor(waccv, sp8, -1e30, waccv, ALU.mult, ALU.add))
        s4 = r_sm[:, 2368:2432]
        s43 = s4.rearrange("p (j e) -> p j e", e=1)
        P.emit("ve", lambda e: e.reduce_sum(s43, cwv, axis=AX.X))
        P.emit("ve", lambda e: e.tensor_scalar_add(s4, s4, 1e-20), inc=sVE)
        s4_done = sVE.n
        inv = r_sm[:, 2432:2496]
        P.wait("sc", sVE, s4_done)
        P.emit("sc", lambda e: e.activation(inv, s4, AF.Ln))
        P.emit("sc", lambda e: e.activation(inv, inv, AF.Exp, scale=-1.0), inc=sSC)
        inv_done = sSC.n
        invb = dataclasses.replace(inv, ap=[inv.ap[0], [1, 64], [0, 8]])
        P.wait("ve", sSC, inv_done)
        P.emit("ve", lambda e: e.scalar_tensor_tensor(cwv, cwv, 2.5, invb, ALU.mult, ALU.mult))
        P.emit("ve", lambda e: e.transpose(cwp16[:], cwpad[:]))
        P.emit("ve", lambda e: e.tensor_copy(self.cw_fm[:], cwp16[:]), inc=sVE)
        self.routing_done = sVE.n
        if PROBE_ROUTE:
            P.wait("g", sVE, self.routing_done)
            P.dma("g", self.pr_sco, scoT[:], self.sOUT)
            P.dma("g", self.pr_scc, scc[:], self.sOUT)
            P.dma("g", self.pr_rsm, r_sm[:], self.sOUT)
            P.dma("g", self.pr_lgt, lgT[:], self.sOUT)
            P.dma("g", self.pr_gw, gw_sb[:], self.sOUT)

    def p1_attention(self, es):
        nc, P = self.nc, self.P
        sG, sT, sVE = self.sG, self.sT, self.sVE
        t = lambda name, shape, dty: es.enter_context(nc.sbuf_tensor(name, shape, dty))
        self.Vb = t("Vb", [128, NKT * 520], BF16)
        self.kcs = (t("kcs0", [128, S], BF16), t("kcs1", [128, S], BF16))
        self.qcs = (t("qcs0", [128, NTOK], BF16), t("qcs1", [128, NTOK], BF16))
        self.ptb = t("ptb", [128, 4 * 1024], BF16)
        self.wpair = t("wpair", [128, 4 * 8 * 128], BF16)
        self.tscr = t("tscr", [128, 9 * 512], BF16)
        self.qcsc = t("qcsc", [128, 2 * 2048], BF16)
        self.msrow = t("msrow", [2, 2 * 512], F32)
        self.msbf = t("msbf", [2, 1024], BF16)
        self.avn = self.tscr[0:64, 2560:3584]
        self.rrow_f = self.msrow[0:1, 0:1024]
        self.rrow = self.msbf[0:1, 512:1024]
        self.avn_dma = 0
        self.wpair_v = self.wpair[:].rearrange("p (m kc c) -> p m kc c", m=4, kc=8)

        P.wait("t", sVE, self.ln_done)
        P.wait("t", self.sSC, self.sSC.n)
        P.wait("ve", sVE, self.c2_done)
        P.wait("g", sVE, self.routing_done)
        P.wait("g", self.sSC, self.sSC.n)
        P.dma("g", self.wpair[:].rearrange("p (m z) -> p m z", m=4),
              self.wpk[0].rearrange("m p z -> p m z"), sG)
        self.wpair_ld = sG.n
        self.merged = 0
        self.merged_g = 0
        self.avn_evac = 0
        self.rrep_dma = 0
        self.rsc_i = 0
        self.avm_prev = 0
        self.pst = dict(amp_evac={}, phi_evac={}, rb_used=0, mrow_done=0, qcs_mult=0,
                        ms_read={})
        self.mcol_sy = {}
        self.mcol_g = {}
        prev_vmm = 0
        for grp in range(DEV_NGRP):
            with nc.sbuf_tensor(self._pn("wv_sb"), [128, 8 * 512], BF16) as wv_sb:
                wv_v = wv_sb[:].rearrange("p (kc c) -> p kc c", kc=8)
                P.wait("g", sT, prev_vmm)
                P.dma("g", wv_sb[:], self.wvt_h[grp], sG)
                wv_ld = sG.n
                if grp == 0:
                    vb_ones = self.Vb[:].rearrange("p (s2 h c) -> p s2 h c",
                                                   s2=NKT, h=8)[:, :, :, 64:65]
                    P.wait("ve", sT, sT.n)
                    P.emit("ve", lambda e, a=vb_ones: e.memset(a, 1.0), inc=sVE)
                    ones_done = sVE.n
                else:
                    P.wait("ve", sT, sT.n)
                pv_evac = {0: 0, 1: 0}
                with (
                    nc.psum_tensor(self._pn("p_v0"), [128, 512], F32) as p_v0,
                    nc.psum_tensor(self._pn("p_v1"), [128, 512], F32) as p_v1,
                ):
                    p_vs = (p_v0, p_v1)
                    if DEV_VB:
                        P.wait("t", sG, wv_ld)
                        P.wait("t", sVE, ones_done)
                        for s_ in range(NKT):
                            src = self.xn_src[s_ // 16]
                            tcol = (s_ % 16) * 128
                            p_v = p_vs[s_ % 2]
                            P.wait("t", sVE, pv_evac[s_ % 2])
                            for kc in range(8):
                                P.mm(p_v[:], src[:, kc, :][:, tcol:tcol + 128],
                                     wv_v[:, kc, :],
                                     kc == 0, kc == 7, inc=(sT if kc == 7 else None))
                            vmm = sT.n
                            dst = self.Vb[:, s_ * 520:s_ * 520 + 520] \
                                .rearrange("p (h c) -> p h c", h=8)[:, :, 0:64]
                            psrc = p_v[:].rearrange("p (h c) -> p h c", h=8)
                            P.wait("ve", sT, vmm)
                            pv_evac[s_ % 2] = P.emit("ve", lambda e, d=dst, s2=psrc:
                                                     e.tensor_copy(d, s2), inc=sVE)
                        prev_vmm = vmm
            self.vb_done = max(pv_evac.values())
            if DEV_ATTN >= 2:
                for pig in range(DEV_NPIG):
                    self.p1_pair(grp, pig)
        self.av_dma_all = self.sSY.n

    def p1_pair(self, grp, pig):
        nc, P = self.nc, self.P
        sG, sT, sSY = self.sG, self.sT, self.sSY
        sSC, sVE = self.sSC, self.sVE
        pair = grp * 4 + pig
        with (
            nc.psum_tensor(self._pn("p_ms"), [2, 512], F32) as p_ms,
            nc.psum_tensor(self._pn("p_rb"), [128, 512], F32) as p_rb,
        ):
            state = self.pst
            pend = None
            with (
                nc.psum_tensor(self._pn("p_a"), [128, 3, 512], F32) as p_a,
                nc.psum_tensor(self._pn("p_p"), [128, 3, 512], F32) as p_p,
            ):
                P.wait("t", sG, self.wpair_ld)
                P.wait("t", sVE, self.vb_done)
                for (isq, si) in ((True, 0), (False, 0), (False, 1))[0:DEV_SIDES]:
                    src = self.xn_own if isq else self.xn_src[si]
                    for cp in range(DEV_CHUNKS // 2):
                        c0 = self.p1_proj_chunk(pair, isq, si, src, 2 * cp,
                                                p_a, p_p, p_ms, p_rb, state)
                        c1 = self.p1_proj_chunk(pair, isq, si, src, 2 * cp + 1,
                                                p_a, p_p, p_ms, p_rb, state)
                        if pend is not None:
                            pend[0][2]()
                            pend[0][3]()
                            pend[1][2]()
                            pend[1][3]()
                        c0[0]()
                        c1[0]()
                        c0[1]()
                        c1[1]()
                        pend = (c0, c1)
                self.proj_mm_last = self.sT.n
            with (
                nc.psum_tensor(self._pn("p_dm"), [128, 512], F32) as _p_dm,
                nc.psum_tensor(self._pn("p_fl"), [128, 512], F32) as p_fl,
            ):
                for _ in range(12):
                    P.mm(p_fl[:], self.masks[:, 0:128], self.xn_own[:, 0, 0:512],
                         True, True)
                pend[0][2]()
                pend[0][3]()
                pend[1][2]()
                pend[1][3]()
                pend = None
        if pair + 1 < NPAIR:
            P.wait("g", sT, self.proj_mm_last)
            P.dma("g", self.wpair[:].rearrange("p (m z) -> p m z", m=4),
                  self.wpk[pair + 1].rearrange("m p z -> p m z"), sG)
            self.wpair_ld = sG.n
        if DEV_ATTN >= 3:
            self.last_exp = {0: 0, 1: 0}
            for j0 in (0, 4, 8, 12):
                self.p1_strip(pair, pig, j0)

    def p1_proj_chunk(self, pair, isq, si, src, ch, p_a, p_p, p_ms, p_rb, st_):
        P = self.P
        sT, sSC, sVE = self.sT, self.sSC, self.sVE
        tscr = self.tscr
        m_a = 0 if isq else 2
        m_p = 1 if isq else 3
        qc = self.qcsc[:, 0:2048]
        qs = self.qcsc[:, 2048:4096]
        par = ch % 2
        par3 = ch % 3
        cols = slice(ch * 512, ch * 512 + 512)
        escr = tscr[:, 0:512]; csb = escr
        ampb = tscr[:, (1 + par) * 512:(2 + par) * 512]
        asq = tscr[:, (3 + par) * 512:(4 + par) * 512]
        tnh = tscr[:, (5 + par) * 512:(6 + par) * 512]
        snb = tscr[:, (7 + par) * 512:(8 + par) * 512]
        pa = p_a[:, par3, :]
        pp = p_p[:, par3, :]
        P.wait("t", sSC, st_["amp_evac"].get(par3, 0))
        for kc in range(8):
            P.mm(pa, self.wpair_v[:, m_a, kc, :], src[:, kc, :][:, cols],
                 kc == 0, kc == 7, inc=(sT if kc == 7 else None))
        amp_mm = sT.n
        P.wait("t", sSC, st_["phi_evac"].get(par3, 0))
        for kc in range(8):
            P.mm(pp, self.wpair_v[:, m_p, kc, :], src[:, kc, :][:, cols],
                 kc == 0, kc == 7, inc=(sT if kc == 7 else None))
        phi_mm = sT.n
        cell = {}

        def soft():
            P.wait("sc", sT, amp_mm)
            P.wait("sc", sVE, st_["qcs_mult"])
            P.wait("sc", self.sSY, self.avn_dma)
            P.emit("sc", lambda e: e.activation(escr, pa, AF.Exp))
            st_["amp_evac"][par3] = P.emit("sc", lambda e:
                                           e.activation(ampb, escr, AF.Ln,
                                                        bias=self.onecol_f[:]),
                                           inc=sSC)
            P.wait("ve", sSC, st_["amp_evac"][par3])
            P.wait("ve", sT, st_["ms_read"].get(par, 0))
            P.emit("ve", lambda e: e.tensor_tensor(asq, ampb, ampb, ALU.mult), inc=sVE)
            cell["asq_done"] = self.sVE.n

        def trig():
            P.wait("sc", sT, phi_mm)
            tanh_d = P.emit("sc", lambda e: e.activation(tnh, pp, AF.Tanh), inc=sSC)
            st_["phi_evac"][par3] = tanh_d
            P.wait("ve", sSC, tanh_d)
            P.wait("ve", self.sSY, self.mcol_sy.get(ch, 0))
            P.wait("ve", self.sG, self.mcol_g.get(ch, 0))
            abs_d = P.emit("ve", lambda e:
                           e.scalar_tensor_tensor(qc[:, cols], tnh, -1.0, tnh,
                                                  ALU.mult, ALU.max), inc=sVE)
            P.emit("sc", lambda e: e.activation(snb, tnh, AF.Sin, scale=PI))
            P.wait("sc", sVE, abs_d)
            P.emit("sc", lambda e:
                   e.activation(tnh, qc[:, cols], AF.Sin, scale=-PI,
                                bias=self.pio2col[:]),
                   inc=sSC)
            cell["cos_done"] = sSC.n

        def stats():
            P.wait("t", sVE, cell["asq_done"])
            P.wait("t", sSC, st_.get("pms_read", 0))
            st_["ms_read"][par] = P.mm(p_ms[:], self.bones[:], asq, True, True, inc=sT)
            ms_mm = sT.n
            mrow = self.msrow[0:2, 0:512]
            P.wait("sc", sT, ms_mm)
            P.wait("sc", sT, st_["rb_used"])
            P.emit("sc", lambda e: e.activation(mrow, p_ms[:], AF.Ln,
                                                scale=1.0 / 64, bias=self.epscol2[:]))
            bias_c = self.zero2[:] if isq else self.lnc2[:, pair:pair + 1]
            P.emit("sc", lambda e, b=bias_c:
                   e.activation(self.msbf[0:2, 0:512], mrow, AF.Exp,
                                scale=-0.5, bias=b), inc=sSC)
            rstd_done = sSC.n
            st_["pms_read"] = rstd_done
            P.wait("t", sSC, rstd_done)
            P.wait("t", sVE, st_["qcs_mult"])
            st_["rb_used"] = P.mm(p_rb[:], self.bonesT[:], self.msbf[0:2, 0:512],
                                  True, True, inc=sT)

        def qcs_tail():
            P.wait("ve", sSC, cell["cos_done"])
            P.wait("ve", sT, st_["rb_used"])
            P.emit("ve", lambda e: e.tensor_tensor(qc[:, cols], ampb, p_rb[:], ALU.mult))
            P.emit("ve", lambda e: e.tensor_tensor(qs[:, cols], qc[:, cols], snb, ALU.mult))
            P.emit("ve", lambda e: e.tensor_tensor(qc[:, cols], qc[:, cols], tnh, ALU.mult),
                   inc=sVE)
            st_["qcs_mult"] = sVE.n
            if DEV_MERGE:
                dsts, off = (self.qcs, 0) if isq else (self.kcs, si * NTOK)
                P.wait("sy", sVE, st_["qcs_mult"])
                if DEV_MQ:
                    P.wait("g", sVE, st_["qcs_mult"])
                dc = slice(off + ch * 512, off + ch * 512 + 512)
                for h in range(2):
                    P.dma("sy", dsts[h][0:64, dc],
                          self.qcsc[64 * h:64 * h + 64, cols], self.sSY)
                    if DEV_MQ:
                        self.merged_g = P.dma(
                            "g", dsts[h][64:128, dc],
                            self.qcsc[64 * h:64 * h + 64,
                                      2048 + ch * 512:2048 + ch * 512 + 512], self.sG)
                    else:
                        P.dma("sy", dsts[h][64:128, dc],
                              self.qcsc[64 * h:64 * h + 64,
                                        2048 + ch * 512:2048 + ch * 512 + 512], sSY)
                self.merged = self.sSY.n
                self.mcol_sy[ch] = self.merged
                self.mcol_g[ch] = self.merged_g
        return soft, trig, stats, qcs_tail

    def p1_strip(self, pair, pig, j0):
        nc, P = self.nc, self.P
        sT, sSC, sVE, sSY, sG = self.sT, self.sSC, self.sVE, self.sSY, self.sG
        qcols = slice(j0 * 128, j0 * 128 + 512)
        if self.cast_jobs:
            dstc, srcc = self.cast_jobs.pop(0)
            P.dma("g", dstc, srcc, sG)
        with (
            nc.psum_tensor(self._pn("p_av0"), [65, 512], F32) as p_av0,
            nc.psum_tensor(self._pn("p_av1"), [65, 512], F32) as p_av1,
            nc.psum_tensor(self._pn("p_fls"), [128, 512], F32) as p_fls,
        ):
            p_av = (p_av0, p_av1)

            def fill(n):
                for _ in range(n):
                    P.mm(p_fls[:], self.masks[:, 0:128], self.xn_own[:, 0, 0:512],
                         True, True)
            with (
                nc.psum_tensor(self._pn("p_s0"), [128, 2, 512], F32) as p_s0,
                nc.psum_tensor(self._pn("p_s1"), [128, 2, 512], F32) as p_s1,
            ):
                p_s = (p_s0, p_s1)
                P.wait("t", sSY, self.merged)
                P.wait("t", sG, self.merged_g)
                P.wait("t", sVE, self.vb_done)
                P.wait("t", sVE, self.avm_prev)
                P.wait("t", sSC, self.avn_evac)
                n_u = j0 + 4
                exp_done = {}
                mask_done = {}
                av_done = {}
                for u in range(n_u + 1):
                    if u < n_u:
                        w0 = max(0, u - j0) * 128
                        for h in range(2):
                            ps = p_s[h]
                            par = u % 2
                            po = self.ptb[:, (h * 2 + par) * 1024:
                                          (h * 2 + par) * 1024 + 1024] \
                                .rearrange("p (s c) -> p s c", s=2)
                            head = pair * 2 + h
                            P.wait("t", sSC, exp_done.get((u - 1, h), self.last_exp[h]))
                            for src in range(2):
                                slot = src * 16 + u
                                P.mm(ps[:, src, w0:512],
                                     self.kcs[h][:, slot * 128:slot * 128 + 128],
                                     self.qcs[h][:, qcols][:, w0:512],
                                     True, True, inc=(sT if src == 1 else None))
                            s_mm = sT.n
                            P.wait("sc", sT, s_mm)
                            exp_done[(u, h)] = P.emit(
                                "sc", lambda e, ps=ps, po=po, w0=w0, head=head:
                                e.activation(po[:, :, w0:512], ps[:, :, w0:512], AF.Exp,
                                             bias=self.mcols[:, head:head + 1]), inc=sSC)
                            if u >= j0:
                                P.wait("ve", sSC, exp_done[(u, h)])
                                for src in range(2):
                                    P.emit("ve", lambda e, po=po, w0=w0, src=src:
                                           e.tensor_tensor(po[:, src, w0:w0 + 128],
                                                           po[:, src, w0:w0 + 128],
                                                           self.masks[:,
                                                                      src * 128:src * 128 + 128],
                                                           ALU.mult),
                                           inc=(sVE if src == 1 else None))
                                mask_done[(u, h)] = self.sVE.n
                    if u >= 1:
                        ua = u - 1
                        w0 = max(0, ua - j0) * 128
                        for h in range(2):
                            par = ua % 2
                            po = self.ptb[:, (h * 2 + par) * 1024:
                                          (h * 2 + par) * 1024 + 1024] \
                                .rearrange("p (s c) -> p s c", s=2)
                            if ua >= j0:
                                P.wait("t", sVE, mask_done[(ua, h)])
                            for src in range(2):
                                slot = src * 16 + ua
                                vb_l = self.Vb[:, slot * 520 + (pig * 2 + h) * 65:
                                               slot * 520 + (pig * 2 + h) * 65 + 65]
                                st = (ua == 0 and src == 0)
                                sp = (ua == n_u - 1 and src == 1)
                                P.mm(p_av[h][:, w0:512], vb_l, po[:, src, w0:512],
                                     st, sp, inc=(sT if src == 1 else None))
                            if ua == n_u - 1:
                                av_done[h] = sT.n
                self.last_exp = {h: exp_done[(n_u - 1, h)] for h in range(2)}
            for h in range(2):
                with nc.psum_tensor(self._pn("p_bc"), [64, 512], F32) as p_bc:
                    P.wait("sc", sT, av_done[h])
                    P.emit("sc", lambda e, pav=p_av[h]:
                           e.activation(self.rrow_f[:, 0:512], pav[64:65, :], AF.Ln))
                    P.emit("sc", lambda e:
                           e.activation(self.rrow, self.rrow_f[:, 0:512],
                                        AF.Exp, scale=-1.0), inc=sSC)
                    rr_done = sSC.n
                    fill(6)
                    P.wait("t", sSC, rr_done)
                    P.wait("t", sVE, self.avm_prev)
                    P.mm(p_bc[:], self.ones1[:, 0:64], self.rrow,
                         True, True, inc=sT)
                    bc_mm = sT.n
                    P.wait("sc", sT, bc_mm)
                    P.wait("sc", sSY, self.avn_dma)
                    self.avn_evac = P.emit("sc", lambda e, pav=p_av[h]:
                                           e.activation(self.avn[:, 0:512], pav[0:64, :],
                                                        AF.Copy), inc=sSC)
                    P.wait("ve", sSC, self.avn_evac)
                    P.emit("ve", lambda e:
                           e.tensor_tensor(self.avn[:, 0:512], self.avn[:, 0:512],
                                           p_bc[:], ALU.mult), inc=sVE)
                    self.avm_prev = self.sVE.n
                    P.wait("sy", sVE, self.avm_prev)
                    self.avn_dma = P.dma(
                        "sy", self.av_all[:, pair, :][64 * h:64 * h + 64, qcols],
                        self.avn[:, 0:512], sSY)
        self.pst["norm_sc"] = self.sSC.n

    def p3_ffn(self, es):
        nc, P = self.nc, self.P
        sG, sT, sSC, sVE, sSY, sOUT = self.sG, self.sT, self.sSC, self.sVE, self.sSY, self.sOUT
        t = lambda name, shape, dty: es.enter_context(nc.sbuf_tensor(name, shape, dty))
        wo_sb = t("wo_sb", [128, 8 * 1024], BF16)
        wg_sb = t("wg_sb", [128, 2 * 8 * 512], BF16)
        wu_sb = t("wu_sb", [128, 2 * 8 * 512], BF16)
        wd_sb = t("wd_sb", [128, 2 * 4 * 1024], BF16)
        he_e = t("he_e", [128, 2 * 4 * 1024], BF16)
        yacc = t("yacc", [128, 8 * 1024], F32)
        gsil = t("gsil", [128, 2 * 512], BF16)
        hetmp = t("hetmp", [128, 2 * 512], BF16)
        yt = t("yt", [128, 2 * 1024], F32)
        cwb_sb = t("cwb_sb", [128, 2 * 1024], BF16)
        wo_v = wo_sb[:].rearrange("p (ic c) -> p ic c", ic=8)
        wg_v = wg_sb[:].rearrange("p (bb kc c) -> p bb kc c", bb=2, kc=8)
        wu_v = wu_sb[:].rearrange("p (bb kc c) -> p bb kc c", bb=2, kc=8)
        wd_v = wd_sb[:].rearrange("p (bb ic c) -> p bb ic c", bb=2, ic=4)
        he_v = he_e[:].rearrange("p (sb ic c) -> p sb ic c", sb=2, ic=4)
        for q in ("g", "sc", "sy"):
            P.wait(q, sVE, self.sVE.n)
            P.wait(q, self.sSC, self.sSC.n)
            P.wait(q, sSY, self.sSY.n)
            P.wait(q, self.sT, self.sT.n)
        while self.cast_jobs:
            dstc, srcc = self.cast_jobs.pop(0)
            P.dma("g", dstc, srcc, sG)
        cast_all = sG.n
        P.wait("sc", sG, cast_all)
        P.wait("sy", sG, cast_all)
        P.dma("sy", wo_sb[:], self.wob, sSY)
        wo_ld = self.sSY.n

        sched = [(hf, e) for hf in range(2) for e in range(9)]
        wgld, wuld, wdld, gu_last = {}, {}, {}, {}

        def issue_gu(j):
            bb = j % 2
            e = sched[j][1]
            P.wait("g", self.sT, gu_last.get(j - 2, 0))
            wgld[j] = P.dma("g", wg_v[:, bb],
                            self.wgb[e].rearrange("p (kc c) -> p kc c", kc=8), sG)
            P.wait("sc", self.sT, gu_last.get(j - 2, 0))
            wuld[j] = P.dma("sc", wu_v[:, bb],
                            self.wub[e].rearrange("p (kc c) -> p kc c", kc=8), self.sLC)

        def issue_wd(j, guard):
            bb = j % 2
            P.wait("sy", self.sT, guard)
            wdld[j] = P.dma("sy", wd_v[:, bb],
                            self.wdb[sched[j][1]].rearrange("p (ic c) -> p ic c", ic=4),
                            sSY)

        issue_gu(0)
        issue_wd(0, 0)
        issue_gu(1)
        issue_wd(1, 0)
        prev_down = 0
        cwbf_done = self.routing_done
        attn_end_ve = self.sVE.n
        attn_end_sc = self.sSC.n
        with (
            nc.psum_tensor("p_g", [128, 2, 512], F32) as p_g,
            nc.psum_tensor("p_u", [128, 2, 512], F32) as p_u,
            nc.psum_tensor("p_d0", [128, 1024], F32) as p_d0,
            nc.psum_tensor("p_d1", [128, 1024], F32) as p_d1,
        ):
            p_d = (p_d0, p_d1)
            st_ = dict(he_mul={}, down=0, silu_read={}, cwb_evac=0)
            yadd = {0: 0, 1: 0}
            xf_read = {0: 0, 1: 0}
            yt_st = {0: 0, 1: 0}
            eplan = [(0, 1), (2, 3), (4, 5), (6, 7), (8, None)][:DEV_FP]
            for half in range(DEV_HALVES):
                for pi, (ea, eb) in enumerate(eplan):
                    subs = [(0, ea)] + ([(1, eb)] if eb is not None else [])

                    def cwb_pair(pj):
                        for sj, ej in ((0, eplan[pj][0]), (1, eplan[pj][1])):
                            P.wait("t", sVE, max(cwbf_done, attn_end_ve, st_["cwb_evac"],
                                                 *(list(st_["he_mul"].values()) or [0])))
                            P.wait("t", self.sSC, attn_end_sc)
                            for ci in range(2):
                                P.mm(p_g[:, ci, :],
                                     self.basis[:, ej * 128:ej * 128 + 128],
                                     self.cw_fm[0:8, half * 1024 + ci * 512:
                                                half * 1024 + ci * 512 + 512],
                                     True, True, inc=(sT if ci == 1 else None))
                            cwb_mm = sT.n
                            P.wait("ve", sT, cwb_mm)
                            P.emit("ve", lambda e, sj=sj: e.tensor_copy(
                                cwb_sb[:, sj * 1024:sj * 1024 + 1024],
                                p_g[:].rearrange("p a b -> p (a b)")), inc=sVE)
                            st_["cwb_evac"] = self.sVE.n

                    if DEV_NOCWB == 1:
                        pass
                    elif DEV_NOCWB == 2 and pi == 0:
                        cwb_pair(0)
                    elif DEV_CWBP:
                        if pi == 0:
                            cwb_pair(0)
                    elif pi <= 3:
                        cwb_pair(pi)
                    he_ready = 0
                    for sub, e_i in subs:
                        k = half * 9 + e_i
                        bb = k % 2
                        if k + 1 < len(sched) and (k + 1) not in wgld:
                            issue_gu(k + 1)
                        if k + 1 < len(sched) and (k + 1) not in wdld and e_i % 2 == 0:
                            issue_wd(k + 1, prev_down)
                        P.wait("t", sG, wgld[k])
                        P.wait("t", self.sLC, wuld[k])
                        P.wait("ve", sT, st_["down"])
                        for ic in range(4):
                            for tcc in range(2):
                                gpar = (ic * 2 + tcc) % 2
                                ccols = slice(half * 1024 + tcc * 512,
                                              half * 1024 + tcc * 512 + 512)
                                P.wait("t", sVE, st_["he_mul"].get(gpar,
                                                                   st_["cwb_evac"]))
                                P.wait("t", sVE, st_["cwb_evac"])
                                for kc in range(8):
                                    P.mm(p_g[:, gpar, :],
                                         wg_v[:, bb, kc, ic * 128:ic * 128 + 128],
                                         self.xn_own[:, kc, :][:, ccols], kc == 0, kc == 7)
                                    P.mm(p_u[:, gpar, :],
                                         wu_v[:, bb, kc, ic * 128:ic * 128 + 128],
                                         self.xn_own[:, kc, :][:, ccols], kc == 0, kc == 7,
                                         inc=(sT if kc == 7 else None))
                                gu_mm = sT.n
                                gslice = gsil[:, gpar * 512:gpar * 512 + 512]
                                htmp = hetmp[:, gpar * 512:gpar * 512 + 512]
                                P.wait("sc", sT, gu_mm)
                                P.wait("sc", sVE, st_["silu_read"].get(gpar, 0))
                                P.emit("sc", lambda e, g=gslice, pg=p_g[:, gpar, :]:
                                       e.activation(g, pg, AF.Silu), inc=sSC)
                                sil = sSC.n
                                hcols2 = slice(sub * 4096 + ic * 1024 + tcc * 512,
                                               sub * 4096 + ic * 1024 + tcc * 512 + 512)
                                P.wait("ve", sSC, sil)
                                if e_i < 8 and DEV_NOCWB == 0:
                                    P.emit("ve", lambda e, g=gslice, htmp=htmp,
                                           pu=p_u[:, gpar, :]:
                                           e.tensor_tensor(htmp, g, pu, ALU.mult))
                                    st_["he_mul"][gpar] = P.emit(
                                        "ve", lambda e, hc=hcols2, htmp=htmp, tcc=tcc, sub=sub:
                                        e.tensor_tensor(he_e[:, hc], htmp,
                                                        cwb_sb[:, sub * 1024 + tcc * 512:
                                                               sub * 1024 + tcc * 512 + 512],
                                                        ALU.mult), inc=sVE)
                                else:
                                    st_["he_mul"][gpar] = P.emit(
                                        "ve", lambda e, g=gslice, hc=hcols2,
                                        pu=p_u[:, gpar, :]:
                                        e.tensor_tensor(he_e[:, hc], g, pu, ALU.mult),
                                        inc=sVE)
                                st_["silu_read"][gpar] = st_["he_mul"][gpar]
                        gu_last[k] = sT.n
                        he_ready = self.sVE.n
                    if DEV_CWBP and not DEV_NOCWB and pi + 1 <= 3 and pi + 1 < len(eplan):
                        cwb_pair(pi + 1)
                    with_wo = (eb is None)
                    P.wait("t", sSY, max(wdld[half * 9 + ea],
                                         wdld[half * 9 + eb] if eb is not None else 0))
                    for tt in range(8):
                        par = tt % 2
                        pd = p_d[par]
                        P.wait("t", sVE, yadd[par])
                        P.wait("t", sVE, he_ready)
                        for sub, e_i in subs:
                            k = half * 9 + e_i
                            bb = k % 2
                            for ic in range(4):
                                for ci in range(2):
                                    P.mm(pd[:, ci * 512:ci * 512 + 512],
                                         he_v[:, sub, ic, tt * 128:tt * 128 + 128],
                                         wd_v[:, bb, ic, ci * 512:ci * 512 + 512],
                                         (sub == 0 and ic == 0),
                                         (not with_wo) and sub == 1 and ic == 3,
                                         inc=(sT if (sub == len(subs) - 1 and ic == 3
                                                     and ci == 1 and not with_wo)
                                              else None))
                        if with_wo:
                            P.wait("t", sSY, wo_ld)
                            P.wait("t", sSY, self.av_dma_all)
                            for ic in range(8):
                                for ci in range(2):
                                    P.mm(pd[:, ci * 512:ci * 512 + 512],
                                         self.av_all[:, ic, :][:, half * 1024 + tt * 128:
                                                               half * 1024 + tt * 128 + 128],
                                         wo_v[:, ic, ci * 512:ci * 512 + 512],
                                         False, ic == 7,
                                         inc=(sT if (ic == 7 and ci == 1) else None))
                        dmm = sT.n
                        st_["down"] = dmm
                        ya = yacc[:, tt * 1024:tt * 1024 + 1024]
                        P.wait("ve", sT, dmm)
                        if pi == 0:
                            yadd[par] = P.emit("ve", lambda e, ya=ya, pd=pd:
                                               e.tensor_copy(ya, pd[:]), inc=sVE)
                        else:
                            yadd[par] = P.emit("ve", lambda e, ya=ya, pd=pd:
                                               e.tensor_tensor(ya, ya, pd[:], ALU.add),
                                               inc=sVE)
                        if with_wo:
                            yts = yt[:, par * 1024:par * 1024 + 1024]
                            P.wait("sy", sOUT, yt_st[par])
                            P.dma("sy", yts,
                                  self.xrows[(half * 8 + tt) * 128:
                                             (half * 8 + tt) * 128 + 128, :], sSY)
                            xf_ld = sSY.n
                            P.wait("ve", sSY, xf_ld)
                            P.emit("ve", lambda e, ya=ya, yts=yts:
                                   e.tensor_tensor(yts, yts, ya, ALU.add), inc=sVE)
                            P.wait("sy", sVE, self.sVE.n)
                            yt_st[par] = P.dma(
                                "sy", self.y_out[(half * 8 + tt) * 128:
                                                 (half * 8 + tt) * 128 + 128, :],
                                yts, sOUT)
                    prev_down = st_["down"]
                    k_last = half * 9 + (eb if eb is not None else ea)
                    if k_last + 1 < len(sched) and (k_last + 1) not in wdld:
                        issue_wd(k_last + 1, prev_down)


def build_program():
    return _Builder().build()



def kernel(L0_attn_x, L0_attn_Wq_amp, L0_attn_Wk_amp, L0_attn_Wq_phi, L0_attn_Wk_phi,
           L0_attn_Wv, L0_attn_Wo, L0_attn_score_log_scale,
           L0_ffn_gate_w, L0_ffn_correction_bias, L0_ffn_Wg, L0_ffn_Wu, L0_ffn_Wd,
           L0_ffn_Wg_s, L0_ffn_Wu_s, L0_ffn_Wd_s):
    x = np.asarray(L0_attn_x, dtype=np.float32)

    def wt(w):
        return np.ascontiguousarray(np.asarray(w, np.float32).T)

    def sb_tiles(wT, m):
        kc = wT.shape[0] // 128
        return np.ascontiguousarray(wT.reshape(kc, 128, m).transpose(1, 0, 2).reshape(128, kc * m))

    mats = [wt(L0_attn_Wq_amp), wt(L0_attn_Wq_phi), wt(L0_attn_Wk_amp), wt(L0_attn_Wk_phi)]
    wpk = np.empty((NPAIR, 4, 128, 1024), np.float32)
    for pr in range(NPAIR):
        for mi, wT in enumerate(mats):
            wpk[pr, mi] = sb_tiles(np.ascontiguousarray(wT[:, pr * 128:(pr + 1) * 128]), 128)
    wvt_f = wt(L0_attn_Wv)
    wvt = np.stack([sb_tiles(np.ascontiguousarray(wvt_f[:, g * 512:(g + 1) * 512]), 512)
                    for g in range(2)])
    wot = sb_tiles(wt(L0_attn_Wo), 1024)
    gwt = sb_tiles(wt(L0_ffn_gate_w), 8)
    wg = np.stack([sb_tiles(w, 512) for w in
                   list(np.asarray(L0_ffn_Wg, np.float32)) + [np.asarray(L0_ffn_Wg_s, np.float32)]])
    wu = np.stack([sb_tiles(w, 512) for w in
                   list(np.asarray(L0_ffn_Wu, np.float32)) + [np.asarray(L0_ffn_Wu_s, np.float32)]])
    wd = np.stack([sb_tiles(w, 1024) for w in
                   list(np.asarray(L0_ffn_Wd, np.float32)) + [np.asarray(L0_ffn_Wd_s, np.float32)]])
    slsr = np.ascontiguousarray(np.asarray(L0_attn_score_log_scale, np.float32).reshape(1, 16))
    sls2 = np.ascontiguousarray(np.asarray(L0_attn_score_log_scale, np.float32)
                                .reshape(8, 2).T)
    cbr = np.ascontiguousarray(np.asarray(L0_ffn_correction_bias, np.float32).reshape(1, 8))

    tri = (np.arange(128)[None, :] >= np.arange(128)[:, None]).astype(np.float32)
    bones = np.zeros((128, 2), np.float32)
    bones[0:64, 0] = 1.0
    bones[64:128, 1] = 1.0
    bonesT = np.ascontiguousarray(bones.T)
    basis = np.zeros((8, 1024), np.float32)
    for e in range(8):
        basis[e, e * 128:(e + 1) * 128] = 1.0

    nc = build_program()

    in_maps = []
    for core in range(8):
        b, r = core // 2, core % 2
        tiles = x[b].reshape(32, 128, D)
        order = [2 * j + r for j in range(16)] + [2 * j + (1 - r) for j in range(16)]
        xsplit = tiles[order].reshape(S, D)
        im = dict(
            xt=np.ascontiguousarray(xsplit.T),
            xrows=np.ascontiguousarray(xsplit[0:NTOK]),
            masks=np.ascontiguousarray(np.stack(
                [tri, np.zeros((128, 128), np.float32) if r == 0
                 else np.ones((128, 128), np.float32)])),
            wpk=wpk, wvt=wvt, wot=wot, gwt=gwt, wg=wg, wu=wu, wd=wd,
            slsr=slsr, sls2=sls2, cbr=cbr,
            ones1=np.ones((1, 128), np.float32),
            bonesT=bonesT, bones=bones, basis=basis,
            onescol=np.ones((128, 1), np.float32),
            prio4=np.ascontiguousarray(np.broadcast_to(
                np.arange(4, 0, -1, dtype=np.float32)[None, :], (32, 4))),
            prio8=np.ascontiguousarray(np.broadcast_to(
                np.arange(8, 0, -1, dtype=np.float32)[None, :], (32, 8))),
        )
        in_maps.append(im)

    res = run_bass_kernel_spmd(nc, in_maps, core_ids=list(range(8)))
    globals()['LAST_RESULTS'] = res.results

    out = np.empty((B, S, D), np.float32)
    for core in range(8):
        b, r = core // 2, core % 2
        y = res.results[core]["y"].reshape(16, 128, D)
        for j in range(16):
            out[b, (2 * j + r) * 128:(2 * j + r) * 128 + 128, :] = y[j]
    return out
